# version 19
# speedup vs baseline: 1.2617x; 1.0197x over previous
import numpy as np
from contextlib import ExitStack

import concourse.bass as bass
import concourse.mybir as mybir
from concourse.bass_utils import run_bass_kernel_spmd
import concourse.bass_utils as _BU

_orig_gwa = _BU.get_walrus_args

def _gwa(*a, **k):
    return [*_orig_gwa(*a, **k), "--enable-ldw-opt=true"]

_BU.get_walrus_args = _gwa

F32 = mybir.dt.float32
BF16 = mybir.dt.bfloat16
I32 = mybir.dt.int32
AF = mybir.ActivationFunctionType
ALU = mybir.AluOpType
AX = mybir.AxisListType

TILE = 128
D = 2048
DK = 512
H = 16
KVH = 4
GROUPS = H // KVH
NCHUNK = D // TILE
LN_EPS = 1e-5
ATTN_EPS = 1e-6
FFN_EPS = 1e-8
N_CORES = 8

COMPUTE = 1
DMA = 16
STREAMS = ("pe", "act", "dve", "pool", "sp")


class _Op:
    __slots__ = ("stream", "build", "waits", "inc_sem")

    def __init__(self, stream, build, waits, inc_sem):
        self.stream = stream
        self.build = build
        self.waits = waits
        self.inc_sem = inc_sem


class Prog:
    def __init__(self):
        self.ops = []
        self.counts = {s: 0 for s in STREAMS}
        self.synced = {s: {} for s in STREAMS}
        self.writer = {}
        self.readers = {}
        self.dma_cls = None
        self.cls_rings = {"plain": {}, "trans": {}}
        self.fence = {}

    def _need(self, stream, dep_sem, dep_count, waits):
        if dep_count <= 0:
            return
        if dep_sem == stream and stream == "pe":
            return
        if dep_sem not in STREAMS:
            dep_count = self.counts[dep_sem]
        cur = self.synced[stream].get(dep_sem, 0)
        if dep_count <= cur:
            return
        if waits.get(dep_sem, 0) < dep_count:
            waits[dep_sem] = dep_count

    def op(self, stream, build, reads=(), writes=()):
        self._add(stream, stream, COMPUTE, build, reads, writes)

    def dma(self, stream, build, reads=(), writes=(), ring=None, trans=False):
        ring = ring or ("q" + stream)
        cls = "trans" if trans else "plain"
        if cls != self.dma_cls:
            other = "plain" if trans else "trans"
            self.fence = dict(self.cls_rings[other])
            self.dma_cls = cls
        self._add(stream, ring, DMA, build, reads, writes, fence=self.fence)
        self.cls_rings[cls][ring] = self.counts[ring]

    def _add(self, stream, inc_sem, step, build, reads, writes, fence=None):
        waits = {}
        if fence:
            for r, c in fence.items():
                self._need(stream, r, min(c, self.counts.get(r, 0)), waits)
        for t in reads:
            w = self.writer.get(t)
            if w:
                self._need(stream, w[0], w[1], waits)
        for t in writes:
            w = self.writer.get(t)
            if w:
                self._need(stream, w[0], w[1], waits)
            for r in self.readers.get(t, ()):
                self._need(stream, r[0], r[1], waits)
        s = self.synced[stream]
        for sem, val in waits.items():
            s[sem] = max(s.get(sem, 0), val)
        if inc_sem not in self.counts:
            self.counts[inc_sem] = 0
        self.counts[inc_sem] += step
        mark = (inc_sem, self.counts[inc_sem])
        for t in reads:
            self.readers.setdefault(t, []).append(mark)
        for t in writes:
            self.writer[t] = mark
            self.readers[t] = []
        self.ops.append(_Op(stream, build, waits, inc_sem))

    def barrier(self):
        for st in STREAMS:
            waits = {}
            for sem, cnt in self.counts.items():
                if sem == st or cnt == 0:
                    continue
                if self.synced[st].get(sem, 0) < cnt:
                    waits[sem] = cnt
            if waits:
                self.ops.append(_Op(st, None, waits, None))
        for st in STREAMS:
            for sem, cnt in self.counts.items():
                if sem != st:
                    self.synced[st][sem] = cnt

    def wait_all(self, stream):
        waits = {}
        for sem, cnt in self.counts.items():
            if sem == stream or cnt == 0:
                continue
            if self.synced[stream].get(sem, 0) < cnt:
                waits[sem] = cnt
                self.synced[stream][sem] = cnt
        self.ops.append(_Op(stream, None, waits, None))

    def emit(self, nc, ctx):
        sems = {}
        for idx, (name, cnt) in enumerate(self.counts.items()):
            if cnt > 0:
                sems[name] = ctx.enter_context(nc.semaphore(f"sem{idx}"))
        by_stream = {s: [] for s in STREAMS}
        for o in self.ops:
            by_stream[o.stream].append(o)

        def emit_stream(eng, ops):
            for o in ops:
                items = sorted(o.waits.items(), key=lambda kv: str(kv[0]))
                if o.build is None:
                    for sem, val in items:
                        eng.wait_ge(sems[sem], val)
                    continue
                for sem, val in items[:-1]:
                    eng.wait_ge(sems[sem], val)
                inst = o.build(eng)
                if items:
                    sem, val = items[-1]
                    inst._wait_ge(sems[sem], val)
                if o.inc_sem is not None:
                    inst.then_inc(sems[o.inc_sem],
                                  COMPUTE if o.inc_sem in STREAMS else DMA)

        with nc.Block() as block:
            if by_stream["sp"]:
                @block.sync
                def _(e, _o=by_stream["sp"]):
                    emit_stream(e, _o)
            if by_stream["pe"]:
                @block.tensor
                def _(e, _o=by_stream["pe"]):
                    emit_stream(e, _o)
            if by_stream["dve"]:
                @block.vector
                def _(e, _o=by_stream["dve"]):
                    emit_stream(e, _o)
            if by_stream["act"]:
                @block.scalar
                def _(e, _o=by_stream["act"]):
                    emit_stream(e, _o)
            if by_stream["pool"]:
                @block.gpsimd
                def _(e, _o=by_stream["pool"]):
                    emit_stream(e, _o)


def build_program(NT=4096, SEQ=2048, SUP=512, stages="1345", dbg=False):
    nT = NT // TILE
    nSeq = NT // SEQ
    tps = SEQ // TILE
    sps = SEQ // SUP
    tpsup = SUP // TILE
    nSup = NT // SUP
    LAG = 2

    nc = bass.Bass("TRN2", target_bir_lowering=False, debug=False)

    x_in = nc.dram_tensor("x", [NT, D], F32, kind="ExternalInput").ap()
    out = nc.dram_tensor("out", [NT, D], F32, kind="ExternalOutput").ap()
    wffnT = nc.dram_tensor("wffnT", [D, D], F32, kind="ExternalInput").ap()
    wqT = nc.dram_tensor("wqT", [D, D], F32, kind="ExternalInput").ap()
    wkT = nc.dram_tensor("wkT", [D, DK], F32, kind="ExternalInput").ap()
    wvT = nc.dram_tensor("wvT", [D, DK], F32, kind="ExternalInput").ap()
    woT = nc.dram_tensor("woT", [D, D], F32, kind="ExternalInput").ap()
    bq = nc.dram_tensor("bq", [1, D], F32, kind="ExternalInput").ap()
    bk = nc.dram_tensor("bk", [1, DK], F32, kind="ExternalInput").ap()
    bv = nc.dram_tensor("bv", [1, DK], F32, kind="ExternalInput").ap()

    knd = "ExternalOutput" if dbg else "Internal"
    s_xq = nc.dram_tensor("s_xq", [NT, D], BF16, kind=knd).ap()
    s_xn2 = nc.dram_tensor("s_xn2", [NT, D], BF16, kind=knd).ap()
    s_sq = nc.dram_tensor("s_sq", [NT, D], BF16, kind=knd).ap()
    s_attn = nc.dram_tensor("s_attn", [NT, D], BF16, kind=knd).ap()
    s_x1 = nc.dram_tensor("s_x1", [NT, D], F32, kind=knd).ap()
    s_x2 = nc.dram_tensor("s_x2", [NT, D], F32, kind=knd).ap()
    s_xq2 = nc.dram_tensor("s_xq2", [NT, D], BF16, kind=knd).ap()
    s_wffnq = nc.dram_tensor("s_wffnq", [D, D], BF16, kind="Internal").ap()

    p = Prog()
    root = ExitStack()

    cst = root.enter_context(nc.sbuf_tensor("cst", [128, 4], F32))
    nc.const_aps.register(F32, 0.0, cst[:, 0:1])
    ones_bf = root.enter_context(nc.sbuf_tensor("ones_bf", [128, 1], BF16))
    ones_f = root.enter_context(nc.sbuf_tensor("ones_f", [128, 1], F32))
    ones_row = root.enter_context(nc.sbuf_tensor("ones_row", [1, 128], F32))
    kv_sb = root.enter_context(nc.sbuf_tensor("kv_sb", [128, nSeq * DK], BF16))
    ks_cols = root.enter_context(nc.sbuf_tensor("ks_cols", [128, nSeq, KVH], F32))
    idf = root.enter_context(nc.sbuf_tensor("idf", [128, 128], F32))
    idb = root.enter_context(nc.sbuf_tensor("idb", [128, 128], BF16))
    alpha_b = root.enter_context(nc.sbuf_tensor("alpha_b", [128, 2], F32))
    sQ1 = root.enter_context(nc.sbuf_tensor("sQ1", [128, nT], F32))
    sQ2 = root.enter_context(nc.sbuf_tensor("sQ2", [128, nT], F32))

    p.op("dve", lambda e: e.memset(cst[:], 0.0), writes=["cst"])
    p.op("dve", lambda e: e.memset(ones_bf[:], 1.0), writes=["ones_bf"])
    p.op("dve", lambda e: e.memset(ones_f[:], 1.0), writes=["ones_f"])
    p.op("pool", lambda e: e.memset(idf[:], 0.0), writes=["idf"])
    p.op("pool", lambda e: e.affine_select(out=idf[:], in_=idf[:], compare_op=ALU.not_equal,
                                           fill=1.0, base=0, pattern=[[-1, 128]],
                                           channel_multiplier=1),
         reads=["idf"], writes=["idf"])
    p.op("pool", lambda e: e.memset(idb[:], 0.0), writes=["idb"])
    p.op("pool", lambda e: e.affine_select(out=idb[:], in_=idb[:], compare_op=ALU.not_equal,
                                           fill=1.0, base=0, pattern=[[-1, 128]],
                                           channel_multiplier=1),
         reads=["idb"], writes=["idb"])
    p.op("dve", lambda e: e.memset(ones_row[:], 1.0), writes=["ones_row"])

    pro = ExitStack()
    wf = pro.enter_context(nc.sbuf_tensor("wf", [128, NCHUNK * D], F32))
    wqz = pro.enter_context(nc.sbuf_tensor("wqz", [128, D], BF16))
    asums = pro.enter_context(nc.sbuf_tensor("asums", [128, NCHUNK], F32))
    atmp = pro.enter_context(nc.sbuf_tensor("atmp", [128, 1], F32))
    ascl = pro.enter_context(nc.sbuf_tensor("ascl", [1, 2], F32))
    pu = pro.enter_context(nc.sbuf_tensor("pu", [128, D], F32))
    pi = pro.enter_context(nc.sbuf_tensor("pi", [128, D], I32))
    pc = pro.enter_context(nc.sbuf_tensor("pc", [128, D], BF16))
    ps_a = pro.enter_context(nc.psum_tensor("ps_a", [1, 1], F32))
    ps_b = pro.enter_context(nc.psum_tensor("ps_b", [128, 1], F32))

    for i in range(NCHUNK):
        p.dma("sp", lambda e, i=i: e.dma_start(
            out=wf[:, i * D:(i + 1) * D], in_=wffnT[i * 128:(i + 1) * 128, :]),
            writes=[("wf", i)], ring=("wf", i))
        p.op("act", lambda e, i=i: e.activation(
            pu[:], wf[:, i * D:(i + 1) * D], AF.Abs, accum_out=asums[:, i:i + 1]),
            reads=[("wf", i), "cst", "pu"], writes=[("as", i), "pu"])
    p.op("dve", lambda e: e.tensor_reduce(atmp[:], asums[:], AX.X, ALU.add),
         reads=[("as", i) for i in range(NCHUNK)], writes=["atmp"])
    p.op("pe", lambda e: e.matmul(ps_a[:], atmp[:], ones_f[:], start=True, stop=True),
         reads=["atmp", "ones_f"], writes=["ps_a"])
    p.op("dve", lambda e: e.tensor_scalar(ascl[0:1, 0:1], ps_a[:], 1.0 / (D * D),
                                          FFN_EPS, ALU.mult, ALU.max),
         reads=["ps_a"], writes=["ascl0"])
    p.op("dve", lambda e: e.reciprocal(ascl[0:1, 1:2], ascl[0:1, 0:1]),
         reads=["ascl0"], writes=["ascl1"])
    p.op("pe", lambda e: e.matmul(ps_b[:], ones_row[:], ascl[0:1, 0:1],
                                  start=True, stop=True),
         reads=["ones_row", "ascl0"], writes=["ps_b"])
    p.op("dve", lambda e: e.tensor_copy(alpha_b[:, 0:1], ps_b[:]),
         reads=["ps_b"], writes=["alv"])
    p.op("pe", lambda e: e.matmul(ps_b[:], ones_row[:], ascl[0:1, 1:2],
                                  start=True, stop=True),
         reads=["ones_row", "ascl1", "alv"], writes=["ps_b"])
    p.op("dve", lambda e: e.tensor_copy(alpha_b[:, 1:2], ps_b[:]),
         reads=["ps_b"], writes=["arv"])

    for i in range(NCHUNK):
        wfi = wf[:, i * D:(i + 1) * D]
        p.op("dve", lambda e, wfi=wfi: e.tensor_scalar(
            pu[:], wfi, alpha_b[:, 1:2], 1.25, ALU.mult, ALU.min),
            reads=[("wf", i), "arv", "pu", "pc"], writes=["pu"])
        p.op("dve", lambda e: e.tensor_scalar(pu[:], pu[:], -1.25, 0.5, ALU.max, ALU.add),
             reads=["pu"], writes=["pu"])
        p.op("dve", lambda e: e.tensor_copy(pi[:], pu[:]), reads=["pu"], writes=["pi"])
        p.op("dve", lambda e: e.tensor_tensor(pc[:], pi[:], pu[:], ALU.is_gt),
             reads=["pi", "pu"], writes=["pc"])
        p.op("dve", lambda e: e.tensor_tensor(wqz[:], pi[:], pc[:], ALU.subtract),
             reads=["pi", "pc", "wqz"], writes=["wqz"])
        p.dma("sp", lambda e, i=i: e.dma_start(
            out=s_wffnq[i * 128:(i + 1) * 128, :], in_=wqz[:]),
            reads=["wqz"], ring="sw")
    p.barrier()
    pro.close()

    def load_weight_scratch_bf16(stk, wname, dram_bf, cols):
        wsb = stk.enter_context(nc.sbuf_tensor(wname, [128, NCHUNK * cols], BF16))
        for i in range(NCHUNK):
            p.dma("sp", lambda e, i=i: e.dma_start(
                out=wsb[:, i * cols:(i + 1) * cols],
                in_=dram_bf[i * 128:(i + 1) * 128, :]),
                writes=[(wname, i)], ring=("wld", i))
        return wsb

    def load_weight_bf16(stk, wname, dram, cols, stage, stage_cols):
        wsb = stk.enter_context(nc.sbuf_tensor(wname, [128, NCHUNK * cols], BF16))
        for i in range(NCHUNK):
            slot = (i % 2) if stage_cols else 0
            sl = stage[:, slot * stage_cols:slot * stage_cols + cols]
            p.dma("sp", lambda e, sl=sl, i=i: e.dma_start(
                out=sl, in_=dram[i * 128:(i + 1) * 128, :]),
                writes=[("wstg", stage.name, slot)], ring=f"ws{slot}")
            p.op("dve", lambda e, sl=sl, i=i: e.tensor_copy(
                wsb[:, i * cols:(i + 1) * cols], sl),
                reads=[("wstg", stage.name, slot)], writes=[(wname, i)])
        return wsb

    def loadT_sup(s, dram_bf, stage, tok_st, dram_toks, ring, reads_extra=()):
        t0 = s * SUP
        for i in range(NCHUNK):
            p.dma("sp", lambda e, i=i, t0=t0: e.dma_start(
                out=stage[:, i, :],
                in_=dram_bf[t0:t0 + SUP, i * 128:(i + 1) * 128], transpose=True),
                reads=list(dram_toks) + list(reads_extra),
                writes=[tok_st + (s % 2, i)], ring=f"{ring}{s % 2}", trans=True)

    def quant_load(t, ten, sname, src_dram, ring_load):
        kk = t % (2 * tpsup)
        p.dma("sp", lambda e, t=t, kk=kk: e.dma_start(
            out=ten["x"][kk][:], in_=src_dram[t * TILE:(t + 1) * TILE, :]),
            writes=[(sname, "x", kk)], ring=f"{ring_load}{kk}")

    def quant_A(t, ten, sname, src_dram, ring_load):
        k = t % tpsup
        kk = t % (2 * tpsup)
        xbuf = ten["x"][kk]
        xtok = ten["bn"]
        pp = (t // tpsup) % 2
        tok = (sname, "x", kk)
        SA = (sname, "stA", pp)
        ub = ten["asc"]
        p.op("act", lambda e, pp=pp, k=k: e.activation(
            ub[:], xbuf[:], AF.Copy, accum_out=ten["mv"][:, pp, k, 0:1]),
            reads=[tok, "cst", (sname, "asc")],
            writes=[SA, (sname, "asc")])
        p.op("act", lambda e, pp=pp, k=k: e.activation(
            ub[:], xbuf[:], AF.Square, accum_out=ten["mv"][:, pp, k, 1:2]),
            reads=[tok, "cst", (sname, "asc")],
            writes=[SA, (sname, "asc")])
        p.op("dve", lambda e, pp=pp, k=k: e.tensor_reduce(
            ten["mx"][:, pp, k:k + 1], xbuf[:], AX.X, ALU.max),
            reads=[tok, SA], writes=[SA])
        p.op("dve", lambda e, pp=pp, k=k: e.tensor_reduce(
            ten["mn"][:, pp, k:k + 1], xbuf[:], AX.X, ALU.min),
            reads=[tok, SA], writes=[SA])

    def quant_B(s, ten, sQ, sname):
        pp = s % 2
        SA = (sname, "stA", pp)
        SB = (sname, "stB", pp)
        mv, mx, mn = ten["mv"], ten["mx"], ten["mn"]
        bt, qs, qb = ten["beta"], ten["qs"], ten["qb"]
        mean = mv[:, pp, :, 0:1]
        var = mv[:, pp, :, 1:2]
        meanf = mean.rearrange("p a b -> p (a b)")
        varf = var.rearrange("p a b -> p (a b)")
        m2 = ten["m2"][:, pp, :]
        p.op("dve", lambda e: e.tensor_scalar(meanf, meanf, 1.0 / D, None, ALU.mult),
             reads=[SA, SB], writes=[SB])
        p.op("dve", lambda e: e.tensor_scalar(varf, varf, 1.0 / D, None, ALU.mult),
             reads=[SA, SB], writes=[SB])
        p.op("dve", lambda e: e.tensor_tensor(m2, meanf, meanf, ALU.mult),
             reads=[SB], writes=[SB])
        p.op("dve", lambda e: e.tensor_tensor(varf, varf, m2, ALU.subtract),
             reads=[SB], writes=[SB])
        p.op("dve", lambda e: e.tensor_scalar(var, var, LN_EPS, None, ALU.add),
             reads=[SB], writes=[SB])
        p.op("act", lambda e: e.activation(var, var, AF.Sqrt),
             reads=[SB, "cst"], writes=[SB])
        p.op("dve", lambda e: e.reciprocal(var, var), reads=[SB], writes=[SB])
        p.op("dve", lambda e: e.tensor_tensor(mx[:, pp, :], mx[:, pp, :], meanf,
                                              ALU.subtract),
             reads=[SA, SB], writes=[SB])
        p.op("dve", lambda e: e.tensor_tensor(mn[:, pp, :], meanf, mn[:, pp, :],
                                              ALU.subtract),
             reads=[SA, SB], writes=[SB])
        p.op("dve", lambda e: e.tensor_tensor(mx[:, pp, :], mx[:, pp, :], mn[:, pp, :],
                                              ALU.max),
             reads=[SB], writes=[SB])
        p.op("dve", lambda e: e.tensor_tensor(bt[:, pp, :], mx[:, pp, :], varf, ALU.mult),
             reads=[SB], writes=[SB])
        p.op("dve", lambda e: e.tensor_scalar(bt[:, pp, :], bt[:, pp, :],
                                              1.0 / 127.0, 1e-8, ALU.mult, ALU.max),
             reads=[SB], writes=[SB])
        p.op("dve", lambda e: e.reciprocal(qs[:, pp, :], bt[:, pp, :]),
             reads=[SB], writes=[SB])
        p.op("dve", lambda e: e.tensor_tensor(qs[:, pp, :], qs[:, pp, :], varf, ALU.mult),
             reads=[SB], writes=[SB])
        p.op("dve", lambda e: e.tensor_tensor(qb[:, pp, :], meanf, qs[:, pp, :], ALU.mult),
             reads=[SB], writes=[SB])
        p.op("dve", lambda e: e.tensor_scalar(qb[:, pp, :], qb[:, pp, :],
                                              -1.0, 0.5, ALU.mult, ALU.add),
             reads=[SB], writes=[SB])
        p.op("dve", lambda e, s=s: e.tensor_scalar(
            sQ[:, s * tpsup:(s + 1) * tpsup], bt[:, pp, :], alpha_b[:, 0:1], 0.5,
            ALU.mult, ALU.mult),
            reads=[SB, "alv"], writes=[(sname, "sQ", s)])

    def quant_C(t, ten, sname):
        k = t % tpsup
        kk = t % (2 * tpsup)
        j = t % 2
        xbuf = ten["x"][kk]
        ub, ib, cb, xqb = ten["u"][j], ten["i"], ten["c"], ten["xq"][t % 4]
        pp = (t // tpsup) % 2
        SB = (sname, "stB", pp)
        tok = (sname, "x", kk)
        p.op("act", lambda e, pp=pp, k=k: e.activation(
            ub[:], xbuf[:], AF.Identity,
            bias=ten["qb"][:, pp, k:k + 1], scale=ten["qs"][:, pp, k:k + 1]),
            reads=[tok, SB, "cst", (sname, "u", j)], writes=[(sname, "u", j)])
        p.op("dve", lambda e: e.tensor_copy(ib[:], ub[:]),
             reads=[(sname, "u", j)], writes=[(sname, "i")])
        p.op("dve", lambda e: e.tensor_tensor(cb[:], ib[:], ub[:], ALU.is_gt),
             reads=[(sname, "i"), (sname, "u", j)], writes=[(sname, "c")])
        p.op("dve", lambda e: e.tensor_tensor(xqb[:], ib[:], cb[:], ALU.subtract),
             reads=[(sname, "i"), (sname, "c"), (sname, "xqb", t % 4)],
             writes=[(sname, "xqb", t % 4)])

    def ffn_local(t, ten, xqT, ptb, wname, wsb, psum, x1b, sQ, sname,
                  dst_dram, ring_out, accsum=None):
        k = t % tpsup
        kk = t % (2 * tpsup)
        j = t % 2
        xqb = ten["xq"][t % 4]
        XT = (sname, "xqT")
        for g in range(2):
            for i2 in range(8):
                i = g * 8 + i2
                p.op("pe", lambda e, i=i, i2=i2: e.transpose(
                    ptb[:, i2 * 128:(i2 + 1) * 128], xqb[:, i * 128:(i + 1) * 128], idb[:]),
                    reads=[(sname, "xqb", t % 4), "idb"],
                    writes=[(sname, "pt", i2)])
            p.op("act", lambda e, g=g: e.activation(
                xqT[:, g * 8:(g + 1) * 8, :].rearrange("p a b -> p (a b)"), ptb[:],
                AF.Copy),
                reads=[(sname, "pt", i2) for i2 in range(8)] + [XT, "cst"],
                writes=[XT] + [(sname, "pt", i2) for i2 in range(8)])
        for i in range(NCHUNK):
            for oc in range(4):
                p.op("pe", lambda e, i=i, oc=oc: e.matmul(
                    psum[:, oc * 512:(oc + 1) * 512],
                    xqT[:, i, :],
                    wsb[:, i * D + oc * 512: i * D + oc * 512 + 512],
                    start=(i == 0), stop=(i == NCHUNK - 1)),
                    reads=[XT, (wname, i)],
                    writes=[(sname, "ps")])
        kw = {}
        wl = [(sname, "x1", t % 2)]
        if accsum is not None:
            kw["accum_out"] = accsum
            wl.append((sname, "lnA"))
        p.op("dve", lambda e, t=t, kk=kk, kw=kw: e.scalar_tensor_tensor(
            out=x1b[:], in0=psum[:], scalar=sQ[:, t:t + 1], in1=ten["x"][kk][:],
            op0=ALU.mult, op1=ALU.add, **kw),
            reads=[(sname, "ps"), (sname, "sQ", t // tpsup), (sname, "x", kk)],
            writes=wl)
        p.dma("sp", lambda e, t=t: e.dma_start(
            out=dst_dram[t * TILE:(t + 1) * TILE, :], in_=x1b[:]),
            reads=[(sname, "x1", t % 2)], writes=[("dram", sname, "o", t)],
            ring=f"{ring_out}{t % 2}")

    def ffn_back(t, s, stage, wname, wsb, psum, xre, x1b, sQ, sname,
                 src_dram, dst_dram, ring_re, ring_out, accsum=None):
        toff = (t % tpsup) * TILE
        p.dma("sp", lambda e, t=t: e.dma_start(
            out=xre[:], in_=src_dram[t * TILE:(t + 1) * TILE, :]),
            writes=[(sname, "xre", t % 2)], ring=f"{ring_re}{t % 2}")
        for i in range(NCHUNK):
            for oc in range(4):
                p.op("pe", lambda e, i=i, oc=oc, toff=toff: e.matmul(
                    psum[:, oc * 512:(oc + 1) * 512],
                    stage[:, i, toff:toff + TILE],
                    wsb[:, i * D + oc * 512: i * D + oc * 512 + 512],
                    start=(i == 0), stop=(i == NCHUNK - 1)),
                    reads=[(sname, "st", s % 2, i), (wname, i)],
                    writes=[(sname, "ps", t % 2)])
        kw = {}
        wl = [(sname, "x1", t % 2)]
        if accsum is not None:
            kw["accum_out"] = accsum
            wl.append((sname, "lnA"))
        p.op("dve", lambda e, t=t, kw=kw: e.scalar_tensor_tensor(
            out=x1b[:], in0=psum[:], scalar=sQ[:, t:t + 1], in1=xre[:],
            op0=ALU.mult, op1=ALU.add, **kw),
            reads=[(sname, "ps", t % 2), (sname, "sQ", t // tpsup), (sname, "xre", t % 2)],
            writes=wl)
        p.dma("sp", lambda e, t=t: e.dma_start(
            out=dst_dram[t * TILE:(t + 1) * TILE, :], in_=x1b[:]),
            reads=[(sname, "x1", t % 2)], writes=[("dram", sname, "o", t)],
            ring=f"{ring_out}{t % 2}")

    if "1" not in stages:
        p.wait_all("sp")
        p.emit(nc, root)
        root.close()
        nc.finalize()
        return nc
    s1 = ExitStack()
    wffn_sb = load_weight_scratch_bf16(s1, "w_s1", s_wffnq, D)

    def mk_front_tensors(stk, pre):
        return {
            "x": [stk.enter_context(nc.sbuf_tensor(f"{pre}x{j}", [128, D], F32))
                  for j in range(2 * tpsup)],
            "bn": stk.enter_context(nc.sbuf_tensor(f"{pre}bn", [128, 24], F32)),
            "u": [stk.enter_context(nc.sbuf_tensor(f"{pre}u{j}", [128, D], F32))
                  for j in range(2)],
            "i": stk.enter_context(nc.sbuf_tensor(f"{pre}i", [128, D], I32)),
            "c": stk.enter_context(nc.sbuf_tensor(f"{pre}c", [128, D], BF16)),
            "xq": [stk.enter_context(nc.sbuf_tensor(f"{pre}xq{j}", [128, D], BF16))
                   for j in range(4)],
            "mv": stk.enter_context(nc.sbuf_tensor(f"{pre}mv", [128, 2, tpsup, 2], F32)),
            "m2": stk.enter_context(nc.sbuf_tensor(f"{pre}m2", [128, 2, tpsup], F32)),
            "mx": stk.enter_context(nc.sbuf_tensor(f"{pre}mx", [128, 2, tpsup], F32)),
            "mn": stk.enter_context(nc.sbuf_tensor(f"{pre}mn", [128, 2, tpsup], F32)),
            "beta": stk.enter_context(nc.sbuf_tensor(f"{pre}bt", [128, 2, tpsup], F32)),
            "qs": stk.enter_context(nc.sbuf_tensor(f"{pre}qs", [128, 2, tpsup], F32)),
            "qb": stk.enter_context(nc.sbuf_tensor(f"{pre}qb", [128, 2, tpsup], F32)),
            "asc": stk.enter_context(nc.sbuf_tensor(f"{pre}asc", [128, D], BF16)),
        }

    ten1 = mk_front_tensors(s1, "s1")
    xqT1 = s1.enter_context(nc.sbuf_tensor("s1xqT", [128, NCHUNK, 128], BF16))
    ptb1 = s1.enter_context(nc.psum_tensor("s1ptb", [128, 1024], BF16))
    x1b1 = [s1.enter_context(nc.sbuf_tensor(f"s1o{j}", [128, D], F32)) for j in range(2)]
    lnsum = s1.enter_context(nc.sbuf_tensor("s1lsu", [128, 2], F32))
    lnss = s1.enter_context(nc.sbuf_tensor("s1lss", [128, 2], F32))
    lnmn = s1.enter_context(nc.sbuf_tensor("s1lmn", [128, 2], F32))
    lnm2 = s1.enter_context(nc.sbuf_tensor("s1lm2", [128, 2], F32))
    lnrs = s1.enter_context(nc.sbuf_tensor("s1lrs", [128, 2], F32))
    lnnb = s1.enter_context(nc.sbuf_tensor("s1lnb", [128, 2], F32))
    xn2b = s1.enter_context(nc.sbuf_tensor("s1xn", [128, D], BF16))
    sqrb = ten1["u"][0]
    psum1 = s1.enter_context(nc.psum_tensor("s1p", [128, D], F32))

    def s1_ln2_pair(t0):
        LA, LB = ("s1", "lnA"), ("s1", "lnB")
        for t in (t0, t0 + 1):
            j = t % 2
            p.op("act", lambda e, j=j: e.activation(
                sqrb[:], x1b1[j][:], AF.Square, accum_out=lnss[:, j:j + 1]),
                reads=[("s1", "x1", j), "cst", ("s1", "u", 0), LA],
                writes=[LA, ("s1", "u", 0)])
        p.op("dve", lambda e: e.tensor_scalar(lnmn[:], lnsum[:], 1.0 / D, None, ALU.mult),
             reads=[LA, LB], writes=[LB])
        p.op("dve", lambda e: e.tensor_scalar(lnss[:], lnss[:], 1.0 / D, None, ALU.mult),
             reads=[LA, LB], writes=[LB])
        p.op("dve", lambda e: e.tensor_tensor(lnm2[:], lnmn[:], lnmn[:], ALU.mult),
             reads=[LB], writes=[LB])
        p.op("dve", lambda e: e.tensor_tensor(lnrs[:], lnss[:], lnm2[:], ALU.subtract),
             reads=[LB], writes=[LB])
        p.op("dve", lambda e: e.tensor_scalar(lnrs[:], lnrs[:], LN_EPS, None, ALU.add),
             reads=[LB], writes=[LB])
        p.op("act", lambda e: e.activation(lnrs[:], lnrs[:], AF.Sqrt),
             reads=[LB, "cst"], writes=[LB])
        p.op("dve", lambda e: e.reciprocal(lnrs[:], lnrs[:]), reads=[LB], writes=[LB])
        p.op("dve", lambda e: e.tensor_tensor(lnnb[:], lnmn[:], lnrs[:], ALU.mult),
             reads=[LB], writes=[LB])
        p.op("dve", lambda e: e.tensor_scalar(lnnb[:], lnnb[:], -1.0, None, ALU.mult),
             reads=[LB], writes=[LB])
        for t in (t0, t0 + 1):
            j = t % 2
            p.op("act", lambda e, j=j: e.activation(
                xn2b[:], x1b1[j][:], AF.Identity, bias=lnnb[:, j:j + 1],
                scale=lnrs[:, j:j + 1]),
                reads=[("s1", "x1", j), LB, "cst", ("s1", "xn2")],
                writes=[("s1", "xn2")])
            p.dma("sp", lambda e, t=t: e.dma_start(
                out=s_xn2[t * TILE:(t + 1) * TILE, :], in_=xn2b[:]),
                reads=[("s1", "xn2")], writes=[("dram", "xn2", t)], ring="sxn")

    for t in range(0, 2 * tpsup):
        quant_load(t, ten1, "s1", x_in, "lx")
    for t in range(0, tpsup):
        quant_A(t, ten1, "s1", x_in, "lx")
    quant_B(0, ten1, sQ1, "s1")
    pend1 = []
    for s in range(nSup):
        for t0 in range(s * tpsup, (s + 1) * tpsup, 2):
            quant_C(t0, ten1, "s1")
            quant_C(t0 + 1, ten1, "s1")
            if pend1:
                pt0 = pend1.pop(0)
                for t in (pt0, pt0 + 1):
                    ffn_local(t, ten1, xqT1, ptb1, "w_s1", wffn_sb, psum1,
                              x1b1[t % 2], sQ1, "s1", s_x1, "sx1",
                              accsum=lnsum[:, t % 2:t % 2 + 1])
                s1_ln2_pair(pt0)
                for t in (pt0 + 2 * tpsup, pt0 + 2 * tpsup + 1):
                    if t < nT:
                        quant_load(t, ten1, "s1", x_in, "lx")
            if s + 1 < nSup:
                for t in (t0 + tpsup, t0 + tpsup + 1):
                    quant_A(t, ten1, "s1", x_in, "lx")
            pend1.append(t0)
        if s + 1 < nSup:
            quant_B(s + 1, ten1, sQ1, "s1")
    while pend1:
        pt0 = pend1.pop(0)
        for t in (pt0, pt0 + 1):
            ffn_local(t, ten1, xqT1, ptb1, "w_s1", wffn_sb, psum1,
                      x1b1[t % 2], sQ1, "s1", s_x1, "sx1",
                      accsum=lnsum[:, t % 2:t % 2 + 1])
        s1_ln2_pair(pt0)
    p.barrier()
    s1.close()

    if "3" not in stages:
        p.wait_all("sp")
        p.emit(nc, root)
        root.close()
        nc.finalize()
        return nc
    s3 = ExitStack()
    bqb = s3.enter_context(nc.sbuf_tensor("bqb", [128, D], F32))
    bkb = s3.enter_context(nc.sbuf_tensor("bkb", [128, DK], F32))
    bvb = s3.enter_context(nc.sbuf_tensor("bvb", [128, DK], F32))
    p.dma("sp", lambda e: e.dma_start(out=bqb[:], in_=bq.partition_broadcast(128)),
          writes=["bqb"], ring="bq")
    p.dma("sp", lambda e: e.dma_start(out=bkb[:], in_=bk.partition_broadcast(128)),
          writes=["bkb"], ring="bk")
    p.dma("sp", lambda e: e.dma_start(out=bvb[:], in_=bv.partition_broadcast(128)),
          writes=["bvb"], ring="bv")
    wstage = s3.enter_context(nc.sbuf_tensor("wstg", [128, 2 * D], F32))
    wq_sb = load_weight_bf16(s3, "wq", wqT, D, wstage, D)
    wk_sb = load_weight_bf16(s3, "wk", wkT, DK, wstage, D)
    wv_sb = load_weight_bf16(s3, "wv", wvT, DK, wstage, D)
    stage3 = [s3.enter_context(nc.sbuf_tensor(f"s3tg{j}", [128, NCHUNK, SUP], BF16))
              for j in range(2)]
    qb_ = s3.enter_context(nc.sbuf_tensor("s3q", [128, D], F32))
    mb_ = s3.enter_context(nc.sbuf_tensor("s3m", [128, D], F32))
    eb_ = s3.enter_context(nc.sbuf_tensor("s3e", [128, D], F32))
    sqb_ = s3.enter_context(nc.sbuf_tensor("s3sq", [128, D], BF16))
    kfb = s3.enter_context(nc.sbuf_tensor("s3kf", [128, DK], F32))
    kmb = s3.enter_context(nc.sbuf_tensor("s3km", [128, DK], F32))
    keb = s3.enter_context(nc.sbuf_tensor("s3ke", [128, DK], F32))
    skb = [s3.enter_context(nc.sbuf_tensor(f"s3sk{j}", [128, DK], BF16))
           for j in range(2)]
    vbf = [s3.enter_context(nc.sbuf_tensor(f"s3v{j}", [128, KVH, 132], BF16))
           for j in range(2)]
    psq = s3.enter_context(nc.psum_tensor("s3pq", [128, D], F32))
    psk = s3.enter_context(nc.psum_tensor("s3pk", [128, DK], F32))
    psv = s3.enter_context(nc.psum_tensor("s3pv", [128, DK], F32))
    pskv = [s3.enter_context(nc.psum_tensor(f"s3pkv{j}", [128, 2, 132], F32))
            for j in range(2)]
    kv_acc = s3.enter_context(nc.sbuf_tensor("kv_acc", [128, nSeq, KVH, 132], F32))

    p.op("dve", lambda e: e.memset(kv_acc[:], 0.0), writes=[("s3", "kva")])
    for j in range(2):
        p.op("dve", lambda e, j=j: e.memset(vbf[j][:], 0.0), writes=[("s3", "v", j)])
        p.op("dve", lambda e, j=j: e.memset(vbf[j][:, :, 128:129], 1.0),
             writes=[("s3", "v", j)])
    wtoks3 = ([("wq", i) for i in range(NCHUNK)] + [("wk", i) for i in range(NCHUNK)]
              + [("wv", i) for i in range(NCHUNK)])
    kv_pend = [None]
    sup_list = list(range(nSup))
    loadT_sup(0, s_xn2, stage3[0], ("s3", "st"), [], "txn", reads_extra=wtoks3)
    for gsup in sup_list:
        if gsup + 1 < nSup:
            loadT_sup(gsup + 1, s_xn2, stage3[(gsup + 1) % 2], ("s3", "st"), [], "txn",
                      reads_extra=(wtoks3 if gsup + 1 < 2 else []))
        sq_ = (gsup * SUP) // SEQ
        for u_loc in range(tpsup):
            t = gsup * tpsup + u_loc
            u = t % tps
            toff = u_loc * TILE
            stg = stage3[gsup % 2]
            for i in range(NCHUNK):
                for oc in range(4):
                    p.op("pe", lambda e, i=i, oc=oc, toff=toff, stg=stg: e.matmul(
                        psq[:, oc * 512:(oc + 1) * 512], stg[:, i, toff:toff + TILE],
                        wq_sb[:, i * D + oc * 512:i * D + oc * 512 + 512],
                        start=(i == 0), stop=(i == NCHUNK - 1)),
                        reads=[("s3", "st", gsup % 2, i), ("wq", i)],
                        writes=[("s3", "psq")])
                p.op("pe", lambda e, i=i, toff=toff, stg=stg: e.matmul(
                    psk[:], stg[:, i, toff:toff + TILE], wk_sb[:, i * DK:(i + 1) * DK],
                    start=(i == 0), stop=(i == NCHUNK - 1)),
                    reads=[("s3", "st", gsup % 2, i), ("wk", i)], writes=[("s3", "psk")])
                p.op("pe", lambda e, i=i, toff=toff, stg=stg: e.matmul(
                    psv[:], stg[:, i, toff:toff + TILE], wv_sb[:, i * DK:(i + 1) * DK],
                    start=(i == 0), stop=(i == NCHUNK - 1)),
                    reads=[("s3", "st", gsup % 2, i), ("wv", i)], writes=[("s3", "psv")])
            if kv_pend[0] is not None:
                kv_pend[0]()
                kv_pend[0] = None
            p.op("dve", lambda e: e.tensor_tensor(qb_[:], psq[:], bqb[:], ALU.add),
                 reads=[("s3", "psq"), "bqb", ("s3", "qf")], writes=[("s3", "qf")])
            p.op("act", lambda e: e.activation(eb_[:], qb_[:], AF.Exp),
                 reads=[("s3", "qf"), "cst", ("s3", "qe")], writes=[("s3", "qe")])
            p.op("dve", lambda e: e.tensor_scalar(mb_[:], eb_[:], 1.0, None, ALU.min),
                 reads=[("s3", "qe"), ("s3", "qm")], writes=[("s3", "qm")])
            p.op("dve", lambda e: e.scalar_tensor_tensor(
                out=sqb_[:], in0=qb_[:], scalar=0.0, in1=mb_[:], op0=ALU.max, op1=ALU.add),
                reads=[("s3", "qf"), ("s3", "qm"), ("s3", "sq")], writes=[("s3", "sq")])
            p.dma("sp", lambda e, t=t: e.dma_start(
                out=s_sq[t * TILE:(t + 1) * TILE, :], in_=sqb_[:]),
                reads=[("s3", "sq")], writes=[("dram", "sq", t)], ring="ssq")
            p.op("dve", lambda e: e.tensor_tensor(kfb[:], psk[:], bkb[:], ALU.add),
                 reads=[("s3", "psk"), "bkb", ("s3", "kf")], writes=[("s3", "kf")])
            p.op("act", lambda e: e.activation(keb[:], kfb[:], AF.Exp),
                 reads=[("s3", "kf"), "cst", ("s3", "ke")], writes=[("s3", "ke")])
            p.op("dve", lambda e: e.tensor_scalar(kmb[:], keb[:], 1.0, None, ALU.min),
                 reads=[("s3", "ke"), ("s3", "km")], writes=[("s3", "km")])
            j2 = t % 2
            p.op("dve", lambda e, j2=j2: e.scalar_tensor_tensor(
                out=skb[j2][:], in0=kfb[:], scalar=0.0, in1=kmb[:],
                op0=ALU.max, op1=ALU.add),
                reads=[("s3", "kf"), ("s3", "km"), ("s3", "sk", j2)],
                writes=[("s3", "sk", j2)])
            p.op("dve", lambda e, j2=j2: e.tensor_tensor(
                vbf[j2][:, :, 0:128], psv[:].rearrange("p (j e) -> p j e", j=KVH),
                bvb[:].rearrange("p (j e) -> p j e", j=KVH), ALU.add),
                reads=[("s3", "psv"), "bvb", ("s3", "v", j2)],
                writes=[("s3", "v", j2)])

            def kv_emit(j2=j2, sq_=sq_):
                for j in range(KVH):
                    p.op("pe", lambda e, j=j, j2=j2: e.matmul(
                        pskv[j // 2][:, j % 2, 0:132], skb[j2][:, j * 128:(j + 1) * 128],
                        vbf[j2][:, j, 0:132], start=True, stop=True),
                        reads=[("s3", "sk", j2), ("s3", "v", j2)],
                        writes=[("s3", "kv", j // 2)])
                for bk_ in range(2):
                    p.op("dve", lambda e, sq_=sq_, bk_=bk_: e.tensor_tensor(
                        kv_acc[:, sq_, 2 * bk_:2 * bk_ + 2, :], pskv[bk_][:],
                        kv_acc[:, sq_, 2 * bk_:2 * bk_ + 2, :], ALU.add),
                        reads=[("s3", "kv", bk_), ("s3", "kva")], writes=[("s3", "kva")])

            if u == tps - 1:
                kv_emit()
                kv_pend[0] = None
            else:
                kv_pend[0] = kv_emit
            if u == tps - 1:
                p.op("dve", lambda e, sq_=sq_: e.tensor_copy(
                    kv_sb[:, sq_ * DK:(sq_ + 1) * DK].rearrange("p (j e) -> p j e", j=KVH),
                    kv_acc[:, sq_, :, 0:128]),
                    reads=[("s3", "kva")], writes=[("kvsb", sq_)])
                p.op("dve", lambda e, sq_=sq_: e.tensor_copy(
                    ks_cols[:, sq_, :], kv_acc[:, sq_, :, 128:129].rearrange(
                        "p j o -> p (j o)")),
                    reads=[("s3", "kva")], writes=[("kssb", sq_)])
    if dbg:
        kvacc_dbg = nc.dram_tensor("kvacc_dbg", [128, nSeq * KVH * 132], F32, kind=knd).ap()
        p.dma("sp", lambda e: e.dma_start(out=kvacc_dbg, in_=kv_acc[:].rearrange("p a b c -> p (a b c)")),
              reads=[("s3", "kva")], ring="dbg1")
    p.barrier()
    s3.close()

    S4WO = "y" not in stages
    S4SEQPRO = "u" not in stages
    if "4" not in stages:
        p.wait_all("sp")
        p.emit(nc, root)
        root.close()
        nc.finalize()
        return nc
    s4 = ExitStack()
    wstage4 = s4.enter_context(nc.sbuf_tensor("wstg4", [128, D], F32))
    if S4WO:
        wo_sb = load_weight_bf16(s4, "w_s4", woT, D, wstage4, 0)
    else:
        wo_sb = s4.enter_context(nc.sbuf_tensor("w_s4", [128, NCHUNK * D], BF16))
    sqT = [s4.enter_context(nc.sbuf_tensor(f"s4qT{j}", [128, NCHUNK, TILE], BF16))
           for j in range(2)]
    stage4a = [s4.enter_context(nc.sbuf_tensor(f"s4ta{j}", [128, NCHUNK, SUP], BF16))
               for j in range(2)]
    rnb = s4.enter_context(nc.sbuf_tensor("s4rn", [128, H], F32))
    attnb = [s4.enter_context(nc.sbuf_tensor(f"s4at{j}", [128, D], BF16))
             for j in range(2)]
    x1l = [s4.enter_context(nc.sbuf_tensor(f"s4x1{j}", [128, D], F32)) for j in range(2)]
    x2b = [s4.enter_context(nc.sbuf_tensor(f"s4x2{j}", [128, D], F32)) for j in range(2)]
    psb = s4.enter_context(nc.psum_tensor("s4pb", [128, 512], F32))
    ksb_b = s4.enter_context(nc.sbuf_tensor("s4ksb", [128, D], F32))
    ksrow = s4.enter_context(nc.sbuf_tensor("s4ksr", [1, KVH, 128], F32))
    sqtm = [s4.enter_context(nc.sbuf_tensor(f"s4sqt{j}", [128, D], BF16)) for j in range(2)]
    nmt = s4.enter_context(nc.sbuf_tensor("s4nmt", [128, D], F32))
    nrm = s4.enter_context(nc.sbuf_tensor("s4nrm", [128, H], F32))
    stage_dbg = nc.dram_tensor("stage_dbg", [128, NCHUNK * SUP], BF16, kind=knd).ap()
    norm_dbg = nc.dram_tensor("norm_dbg", [NT, H], F32, kind=knd).ap()

    def s4_seqpro(sq_):
        for j in range(KVH):
            p.op("pe", lambda e, j=j, sq_=sq_: e.transpose(
                psb[0:1, j * 128:(j + 1) * 128], ks_cols[:, sq_, j:j + 1], idf[:]),
                reads=[("kssb", sq_), "idf", ("s4", "ksr")], writes=[("s4", "psbT", j)])
        p.op("dve", lambda e: e.tensor_copy(
            ksrow[0:1, :, :], psb[0:1, 0:512].rearrange("p (j d) -> p j d", j=KVH)),
            reads=[("s4", "psbT", 0), ("s4", "psbT", 1), ("s4", "psbT", 2),
                   ("s4", "psbT", 3)], writes=[("s4", "ksr")])
        for grp in range(KVH):
            for hh in range(KVH):
                j = (grp * KVH + hh) // GROUPS
                p.op("pe", lambda e, j=j, hh=hh: e.matmul(
                    psb[:, hh * 128:(hh + 1) * 128], ones_row[:],
                    ksrow[0:1, j, :], start=True, stop=True),
                    reads=[("s4", "ksr"), "ones_row", ("s4", "ksbb")],
                    writes=[("s4", "psbB", hh)])
            p.op("act", lambda e, grp=grp: e.activation(
                ksb_b[:, grp * 512:(grp + 1) * 512], psb[:], AF.Copy),
                reads=[("s4", "psbB", 0), ("s4", "psbB", 1), ("s4", "psbB", 2),
                       ("s4", "psbB", 3), "cst"], writes=[("s4", "ksbb")])
    psqk = [s4.enter_context(nc.psum_tensor(f"s4pq{j}", [128, 512], F32)) for j in range(2)]
    psout = s4.enter_context(nc.psum_tensor("s4po", [128, 1024], F32))
    ptb4 = s4.enter_context(nc.psum_tensor("s4ptb", [128, 1024], BF16))

    def s4_sqload(t):
        p.dma("sp", lambda e, t=t: e.dma_start(
            out=sqtm[t % 2][:], in_=s_sq[t * TILE:(t + 1) * TILE, :]),
            writes=[("s4", "sqtm", t % 2)], ring=f"lsq{t % 2}")

    def s4_x1load(t):
        p.dma("sp", lambda e, t=t: e.dma_start(
            out=x1l[t % 2][:], in_=s_x1[t * TILE:(t + 1) * TILE, :]),
            writes=[("s4", "x1l", t % 2)], ring=f"lx1{t % 2}")

    def s4_front(t, gsup, only_mm=False, norm_only=False):
        toff = (t % tpsup) * TILE
        sq_ = t // tps
        sqt = sqT[t % 2]
        for rnd in range(2):
            for i2 in range(8):
                i = rnd * 8 + i2
                p.op("pe", lambda e, i=i, i2=i2, t=t: e.transpose(
                    ptb4[:, i2 * 128:(i2 + 1) * 128],
                    sqtm[t % 2][:, i * 128:(i + 1) * 128], idb[:]),
                    reads=[("s4", "sqtm", t % 2), "idb", ("s4", "pt4", i2)],
                    writes=[("s4", "pt4", i2)])
            p.op("act", lambda e, rnd=rnd, sqt=sqt: e.activation(
                sqt[:, rnd * 8:(rnd + 1) * 8, :],
                ptb4[:].rearrange("p (c t) -> p c t", c=8), AF.Copy),
                reads=[("s4", "pt4", i2) for i2 in range(8)] + ["cst"],
                writes=[("s4", "sqT", t % 2, he) for he in range(rnd * 8, (rnd + 1) * 8)]
                      + [("s4", "pt4", i2) for i2 in range(8)])
        p.op("dve", lambda e, t=t: e.tensor_tensor(
            nmt[:], sqtm[t % 2][:], ksb_b[:], ALU.mult),
            reads=[("s4", "sqtm", t % 2), ("s4", "ksbb"), ("s4", "nmt")],
            writes=[("s4", "nmt")])
        p.op("dve", lambda e: e.tensor_reduce(
            nrm[:], nmt[:].rearrange("p (h d) -> p h d", h=H), AX.X, ALU.add),
            reads=[("s4", "nmt"), ("s4", "nrm")], writes=[("s4", "nrm")])
        if dbg:
            p.dma("sp", lambda e, t=t: e.dma_start(
                out=norm_dbg[t * TILE:(t + 1) * TILE, :], in_=nrm[:]),
                reads=[("s4", "nrm")], ring="dbg3")
        p.op("dve", lambda e: e.tensor_scalar(nrm[:], nrm[:], ATTN_EPS, None, ALU.add),
             reads=[("s4", "nrm")], writes=[("s4", "nrm")])
        p.op("dve", lambda e: e.reciprocal(rnb[:], nrm[:]),
             reads=[("s4", "nrm"), ("s4", "rn")], writes=[("s4", "rn")])
        if norm_only:
            return
        for h in range(H):
            j = h // GROUPS
            bk_ = h % 2
            p.op("pe", lambda e, bk_=bk_, h=h, sqt=sqt, sq_=sq_: e.matmul(
                psqk[bk_][:], sqt[:, h, :],
                kv_sb[:, sq_ * DK:(sq_ + 1) * DK],
                start=True, stop=True),
                reads=[("s4", "sqT", t % 2, h), ("kvsb", sq_)],
                writes=[("s4", "qk", bk_)])
            if bk_ == 0:
                p.op("dve", lambda e, h=h, j=j, bk_=bk_, t=t: e.tensor_scalar(
                    attnb[t % 2][:, h * 128:(h + 1) * 128],
                    psqk[bk_][:, j * 128:(j + 1) * 128],
                    rnb[:, h:h + 1], None, ALU.mult),
                    reads=[("s4", "qk", bk_), ("s4", "rn")],
                    writes=[("s4", "at", t % 2, h)])
            else:
                p.op("act", lambda e, h=h, j=j, bk_=bk_, t=t: e.activation(
                    attnb[t % 2][:, h * 128:(h + 1) * 128],
                    psqk[bk_][:, j * 128:(j + 1) * 128],
                    AF.Identity, scale=rnb[:, h:h + 1]),
                    reads=[("s4", "qk", bk_), ("s4", "rn"), "cst"],
                    writes=[("s4", "at", t % 2, h)])
        if dbg:
            p.dma("sp", lambda e, t=t: e.dma_start(
                out=s_attn[t * TILE:(t + 1) * TILE, :], in_=attnb[t % 2][:]),
                reads=[("s4", "at", t % 2, h2) for h2 in range(H)],
                writes=[("dram", "at", t)], ring="sat")
        stga = stage4a[gsup % 2]
        for rnd in range(2):
            for i2 in range(8):
                i = rnd * 8 + i2
                p.op("pe", lambda e, i=i, i2=i2, t=t: e.transpose(
                    ptb4[:, i2 * 128:(i2 + 1) * 128],
                    attnb[t % 2][:, i * 128:(i + 1) * 128], idb[:]),
                    reads=[("s4", "at", t % 2, i), "idb", ("s4", "pt4", i2)],
                    writes=[("s4", "pt4", i2)])
            p.op("act", lambda e, rnd=rnd, toff=toff, stga=stga: e.activation(
                stga[:, rnd * 8:(rnd + 1) * 8, toff:toff + TILE],
                ptb4[:].rearrange("p (c t) -> p c t", c=8), AF.Copy),
                reads=[("s4", "pt4", i2) for i2 in range(8)] + ["cst"],
                writes=[("s4", "ast", gsup % 2, he) for he in range(rnd * 8, (rnd + 1) * 8)]
                      + [("s4", "pt4", i2) for i2 in range(8)])

    def s4_back(t, sb_):
        toff = (t % tpsup) * TILE
        stg = stage4a[sb_ % 2]
        for half in range(2):
            for he in range(NCHUNK):
                for oc in range(2):
                    p.op("pe", lambda e, he=he, oc=oc, half=half, toff=toff,
                         stg=stg: e.matmul(
                        psout[:, oc * 512:(oc + 1) * 512],
                        stg[:, he, toff:toff + TILE],
                        wo_sb[:, he * D + half * 1024 + oc * 512:
                              he * D + half * 1024 + oc * 512 + 512],
                        start=(he == 0), stop=(he == NCHUNK - 1)),
                        reads=[("s4", "ast", sb_ % 2, he), ("w_s4", he)],
                        writes=[("s4", "po")])
            p.op("dve", lambda e, half=half, t=t: e.tensor_tensor(
                x2b[t % 2][:, half * 1024:(half + 1) * 1024], psout[:],
                x1l[t % 2][:, half * 1024:(half + 1) * 1024], ALU.add),
                reads=[("s4", "po"), ("s4", "x1l", t % 2)],
                writes=[("s4", "x2", t % 2, half)])
        p.dma("sp", lambda e, t=t: e.dma_start(
            out=s_x2[t * TILE:(t + 1) * TILE, :], in_=x2b[t % 2][:]),
            reads=[("s4", "x2", t % 2, 0), ("s4", "x2", t % 2, 1)],
            writes=[("dram", "x2", t)], ring=f"sx2{t % 2}")

    do_loadT = "6" not in stages
    do_back = "7" not in stages
    only_wo = "q" in stages
    only_sqT = "w" in stages
    only_mm = "e" in stages
    wtoks4 = [("w_s4", i) for i in range(NCHUNK)]
    if not only_wo:
        seq_done = set()
        for gsup in range(nSup + 1):
            if gsup < nSup and S4SEQPRO:
                sq_now = (gsup * SUP) // SEQ
                if sq_now not in seq_done:
                    seq_done.add(sq_now)
                    s4_seqpro(sq_now)
            if gsup < nSup and not only_sqT:
                for t0 in range(gsup * tpsup, (gsup + 1) * tpsup, 2):
                    s4_sqload(t0)
                    s4_sqload(t0 + 1)
                    s4_front(t0, gsup, only_mm, norm_only=("z" in stages))
                    s4_front(t0 + 1, gsup, only_mm, norm_only=("z" in stages))

            if do_back and gsup >= 1:
                g2 = gsup - 1
                for t0 in range(g2 * tpsup, (g2 + 1) * tpsup, 2):
                    s4_x1load(t0)
                    s4_x1load(t0 + 1)
                    s4_back(t0, g2)
                    s4_back(t0 + 1, g2)
    p.barrier()
    s4.close()

    if "5" not in stages:
        p.wait_all("sp")
        p.emit(nc, root)
        root.close()
        nc.finalize()
        return nc
    s5 = ExitStack()
    wffn_sb5 = load_weight_scratch_bf16(s5, "w_s5", s_wffnq, D)
    ten5 = mk_front_tensors(s5, "s5")
    xqT5 = s5.enter_context(nc.sbuf_tensor("s5xqT", [128, NCHUNK, 128], BF16))
    ptb5 = s5.enter_context(nc.psum_tensor("s5ptb", [128, 1024], BF16))
    ob5 = [s5.enter_context(nc.sbuf_tensor(f"s5o{j}", [128, D], F32)) for j in range(2)]
    psum5 = s5.enter_context(nc.psum_tensor("s5p", [128, D], F32))

    for t in range(0, 2 * tpsup):
        quant_load(t, ten5, "s5", s_x2, "lx2")
    for t in range(0, tpsup):
        quant_A(t, ten5, "s5", s_x2, "lx2")
    quant_B(0, ten5, sQ2, "s5")
    pend5 = []
    for s in range(nSup):
        for t0 in range(s * tpsup, (s + 1) * tpsup, 2):
            quant_C(t0, ten5, "s5")
            quant_C(t0 + 1, ten5, "s5")
            if pend5:
                pt0 = pend5.pop(0)
                for t in (pt0, pt0 + 1):
                    ffn_local(t, ten5, xqT5, ptb5, "w_s5", wffn_sb5, psum5,
                              ob5[t % 2], sQ2, "s5", out, "sout")
                for t in (pt0 + 2 * tpsup, pt0 + 2 * tpsup + 1):
                    if t < nT:
                        quant_load(t, ten5, "s5", s_x2, "lx2")
            if s + 1 < nSup:
                for t in (t0 + tpsup, t0 + tpsup + 1):
                    quant_A(t, ten5, "s5", s_x2, "lx2")
            pend5.append(t0)
        if s + 1 < nSup:
            quant_B(s + 1, ten5, sQ2, "s5")
    while pend5:
        pt0 = pend5.pop(0)
        for t in (pt0, pt0 + 1):
            ffn_local(t, ten5, xqT5, ptb5, "w_s5", wffn_sb5, psum5,
                      ob5[t % 2], sQ2, "s5", out, "sout")
    p.wait_all("sp")
    s5.close()
    p.emit(nc, root)
    root.close()
    nc.finalize()
    return nc


def kernel(L0_attn_hidden_states, L0_attn_Wq, L0_attn_bq, L0_attn_Wk, L0_attn_bk,
           L0_attn_Wv, L0_attn_bv, L0_attn_Wo, L0_ffn_weight):
    x = np.ascontiguousarray(np.asarray(L0_attn_hidden_states, dtype=np.float32))
    B, S, _ = x.shape
    per = B // N_CORES
    NT = per * S
    nc = build_program(NT=NT, SEQ=S, SUP=512)
    common = {
        "wffnT": np.ascontiguousarray(np.asarray(L0_ffn_weight, np.float32).T),
        "wqT": np.ascontiguousarray(np.asarray(L0_attn_Wq, np.float32).T),
        "wkT": np.ascontiguousarray(np.asarray(L0_attn_Wk, np.float32).T),
        "wvT": np.ascontiguousarray(np.asarray(L0_attn_Wv, np.float32).T),
        "woT": np.ascontiguousarray(np.asarray(L0_attn_Wo, np.float32).T),
        "bq": np.ascontiguousarray(np.asarray(L0_attn_bq, np.float32).reshape(1, -1)),
        "bk": np.ascontiguousarray(np.asarray(L0_attn_bk, np.float32).reshape(1, -1)),
        "bv": np.ascontiguousarray(np.asarray(L0_attn_bv, np.float32).reshape(1, -1)),
    }
    in_maps = []
    for c in range(N_CORES):
        m = dict(common)
        m["x"] = np.ascontiguousarray(x[c * per:(c + 1) * per].reshape(NT, 2048))
        in_maps.append(m)
    res = run_bass_kernel_spmd(nc, in_maps, core_ids=list(range(N_CORES)))
    outs = [np.asarray(res.results[c]["out"]).reshape(per, S, 2048)
            for c in range(N_CORES)]
    return np.concatenate(outs, axis=0)


# version 20
# speedup vs baseline: 1.2689x; 1.0057x over previous
import numpy as np
from contextlib import ExitStack

import concourse.bass as bass
import concourse.mybir as mybir
from concourse.bass_utils import run_bass_kernel_spmd
import concourse.bass_utils as _BU

_orig_gwa = _BU.get_walrus_args

def _gwa(*a, **k):
    return [*_orig_gwa(*a, **k), "--enable-ldw-opt=true"]

_BU.get_walrus_args = _gwa

F32 = mybir.dt.float32
BF16 = mybir.dt.bfloat16
I32 = mybir.dt.int32
AF = mybir.ActivationFunctionType
ALU = mybir.AluOpType
AX = mybir.AxisListType

TILE = 128
D = 2048
DK = 512
H = 16
KVH = 4
GROUPS = H // KVH
NCHUNK = D // TILE
LN_EPS = 1e-5
ATTN_EPS = 1e-6
FFN_EPS = 1e-8
N_CORES = 8

COMPUTE = 1
DMA = 16
STREAMS = ("pe", "act", "dve", "pool", "sp")


class _Op:
    __slots__ = ("stream", "build", "waits", "inc_sem")

    def __init__(self, stream, build, waits, inc_sem):
        self.stream = stream
        self.build = build
        self.waits = waits
        self.inc_sem = inc_sem


class Prog:
    def __init__(self):
        self.ops = []
        self.counts = {s: 0 for s in STREAMS}
        self.synced = {s: {} for s in STREAMS}
        self.writer = {}
        self.readers = {}
        self.dma_cls = None
        self.cls_rings = {"plain": {}, "trans": {}}
        self.fence = {}

    def _need(self, stream, dep_sem, dep_count, waits):
        if dep_count <= 0:
            return
        if dep_sem == stream and stream == "pe":
            return
        if dep_sem not in STREAMS:
            dep_count = self.counts[dep_sem]
        cur = self.synced[stream].get(dep_sem, 0)
        if dep_count <= cur:
            return
        if waits.get(dep_sem, 0) < dep_count:
            waits[dep_sem] = dep_count

    def op(self, stream, build, reads=(), writes=()):
        self._add(stream, stream, COMPUTE, build, reads, writes)

    def dma(self, stream, build, reads=(), writes=(), ring=None, trans=False):
        ring = ring or ("q" + stream)
        cls = "trans" if trans else "plain"
        if cls != self.dma_cls:
            other = "plain" if trans else "trans"
            self.fence = dict(self.cls_rings[other])
            self.dma_cls = cls
        self._add(stream, ring, DMA, build, reads, writes, fence=self.fence)
        self.cls_rings[cls][ring] = self.counts[ring]

    def _add(self, stream, inc_sem, step, build, reads, writes, fence=None):
        waits = {}
        if fence:
            for r, c in fence.items():
                self._need(stream, r, min(c, self.counts.get(r, 0)), waits)
        for t in reads:
            w = self.writer.get(t)
            if w:
                self._need(stream, w[0], w[1], waits)
        for t in writes:
            w = self.writer.get(t)
            if w:
                self._need(stream, w[0], w[1], waits)
            for r in self.readers.get(t, ()):
                self._need(stream, r[0], r[1], waits)
        s = self.synced[stream]
        for sem, val in waits.items():
            s[sem] = max(s.get(sem, 0), val)
        if inc_sem not in self.counts:
            self.counts[inc_sem] = 0
        self.counts[inc_sem] += step
        mark = (inc_sem, self.counts[inc_sem])
        for t in reads:
            self.readers.setdefault(t, []).append(mark)
        for t in writes:
            self.writer[t] = mark
            self.readers[t] = []
        self.ops.append(_Op(stream, build, waits, inc_sem))

    def barrier(self):
        for st in STREAMS:
            waits = {}
            for sem, cnt in self.counts.items():
                if sem == st or cnt == 0:
                    continue
                if self.synced[st].get(sem, 0) < cnt:
                    waits[sem] = cnt
            if waits:
                self.ops.append(_Op(st, None, waits, None))
        for st in STREAMS:
            for sem, cnt in self.counts.items():
                if sem != st:
                    self.synced[st][sem] = cnt

    def wait_all(self, stream):
        waits = {}
        for sem, cnt in self.counts.items():
            if sem == stream or cnt == 0:
                continue
            if self.synced[stream].get(sem, 0) < cnt:
                waits[sem] = cnt
                self.synced[stream][sem] = cnt
        self.ops.append(_Op(stream, None, waits, None))

    def emit(self, nc, ctx):
        sems = {}
        for idx, (name, cnt) in enumerate(self.counts.items()):
            if cnt > 0:
                sems[name] = ctx.enter_context(nc.semaphore(f"sem{idx}"))
        by_stream = {s: [] for s in STREAMS}
        for o in self.ops:
            by_stream[o.stream].append(o)

        def emit_stream(eng, ops):
            for o in ops:
                items = sorted(o.waits.items(), key=lambda kv: str(kv[0]))
                if o.build is None:
                    for sem, val in items:
                        eng.wait_ge(sems[sem], val)
                    continue
                for sem, val in items[:-1]:
                    eng.wait_ge(sems[sem], val)
                inst = o.build(eng)
                if items:
                    sem, val = items[-1]
                    inst._wait_ge(sems[sem], val)
                if o.inc_sem is not None:
                    inst.then_inc(sems[o.inc_sem],
                                  COMPUTE if o.inc_sem in STREAMS else DMA)

        with nc.Block() as block:
            if by_stream["sp"]:
                @block.sync
                def _(e, _o=by_stream["sp"]):
                    emit_stream(e, _o)
            if by_stream["pe"]:
                @block.tensor
                def _(e, _o=by_stream["pe"]):
                    emit_stream(e, _o)
            if by_stream["dve"]:
                @block.vector
                def _(e, _o=by_stream["dve"]):
                    emit_stream(e, _o)
            if by_stream["act"]:
                @block.scalar
                def _(e, _o=by_stream["act"]):
                    emit_stream(e, _o)
            if by_stream["pool"]:
                @block.gpsimd
                def _(e, _o=by_stream["pool"]):
                    emit_stream(e, _o)


def build_program(NT=4096, SEQ=2048, SUP=512, stages="1345", dbg=False):
    nT = NT // TILE
    nSeq = NT // SEQ
    tps = SEQ // TILE
    sps = SEQ // SUP
    tpsup = SUP // TILE
    nSup = NT // SUP
    LAG = 2

    nc = bass.Bass("TRN2", target_bir_lowering=False, debug=False)

    x_in = nc.dram_tensor("x", [NT, D], F32, kind="ExternalInput").ap()
    out = nc.dram_tensor("out", [NT, D], F32, kind="ExternalOutput").ap()
    wffnT = nc.dram_tensor("wffnT", [D, D], F32, kind="ExternalInput").ap()
    wqT = nc.dram_tensor("wqT", [D, D], F32, kind="ExternalInput").ap()
    wkT = nc.dram_tensor("wkT", [D, DK], F32, kind="ExternalInput").ap()
    wvT = nc.dram_tensor("wvT", [D, DK], F32, kind="ExternalInput").ap()
    woT = nc.dram_tensor("woT", [D, D], F32, kind="ExternalInput").ap()
    bq = nc.dram_tensor("bq", [1, D], F32, kind="ExternalInput").ap()
    bk = nc.dram_tensor("bk", [1, DK], F32, kind="ExternalInput").ap()
    bv = nc.dram_tensor("bv", [1, DK], F32, kind="ExternalInput").ap()

    knd = "ExternalOutput" if dbg else "Internal"
    s_xq = nc.dram_tensor("s_xq", [NT, D], BF16, kind=knd).ap()
    s_xn2 = nc.dram_tensor("s_xn2", [NT, D], BF16, kind=knd).ap()
    s_sq = nc.dram_tensor("s_sq", [NT, D], BF16, kind=knd).ap()
    s_attn = nc.dram_tensor("s_attn", [NT, D], BF16, kind=knd).ap()
    s_x1 = nc.dram_tensor("s_x1", [NT, D], F32, kind=knd).ap()
    s_x2 = nc.dram_tensor("s_x2", [NT, D], F32, kind=knd).ap()
    s_xq2 = nc.dram_tensor("s_xq2", [NT, D], BF16, kind=knd).ap()
    s_wffnq = nc.dram_tensor("s_wffnq", [D, D], BF16, kind="Internal").ap()

    p = Prog()
    root = ExitStack()

    cst = root.enter_context(nc.sbuf_tensor("cst", [128, 4], F32))
    nc.const_aps.register(F32, 0.0, cst[:, 0:1])
    ones_bf = root.enter_context(nc.sbuf_tensor("ones_bf", [128, 1], BF16))
    ones_f = root.enter_context(nc.sbuf_tensor("ones_f", [128, 1], F32))
    ones_row = root.enter_context(nc.sbuf_tensor("ones_row", [1, 128], F32))
    kv_sb = root.enter_context(nc.sbuf_tensor("kv_sb", [128, nSeq * DK], BF16))
    ks_cols = root.enter_context(nc.sbuf_tensor("ks_cols", [128, nSeq, KVH], F32))
    idf = root.enter_context(nc.sbuf_tensor("idf", [128, 128], F32))
    idb = root.enter_context(nc.sbuf_tensor("idb", [128, 128], BF16))
    alpha_b = root.enter_context(nc.sbuf_tensor("alpha_b", [128, 2], F32))
    sQ1 = root.enter_context(nc.sbuf_tensor("sQ1", [128, nT], F32))
    sQ2 = root.enter_context(nc.sbuf_tensor("sQ2", [128, nT], F32))

    p.op("dve", lambda e: e.memset(cst[:], 0.0), writes=["cst"])
    p.op("dve", lambda e: e.memset(ones_bf[:], 1.0), writes=["ones_bf"])
    p.op("dve", lambda e: e.memset(ones_f[:], 1.0), writes=["ones_f"])
    p.op("pool", lambda e: e.memset(idf[:], 0.0), writes=["idf"])
    p.op("pool", lambda e: e.affine_select(out=idf[:], in_=idf[:], compare_op=ALU.not_equal,
                                           fill=1.0, base=0, pattern=[[-1, 128]],
                                           channel_multiplier=1),
         reads=["idf"], writes=["idf"])
    p.op("pool", lambda e: e.memset(idb[:], 0.0), writes=["idb"])
    p.op("pool", lambda e: e.affine_select(out=idb[:], in_=idb[:], compare_op=ALU.not_equal,
                                           fill=1.0, base=0, pattern=[[-1, 128]],
                                           channel_multiplier=1),
         reads=["idb"], writes=["idb"])
    p.op("dve", lambda e: e.memset(ones_row[:], 1.0), writes=["ones_row"])

    pro = ExitStack()
    wf = pro.enter_context(nc.sbuf_tensor("wf", [128, NCHUNK * D], F32))
    wqz = pro.enter_context(nc.sbuf_tensor("wqz", [128, D], BF16))
    asums = pro.enter_context(nc.sbuf_tensor("asums", [128, NCHUNK], F32))
    atmp = pro.enter_context(nc.sbuf_tensor("atmp", [128, 1], F32))
    ascl = pro.enter_context(nc.sbuf_tensor("ascl", [1, 2], F32))
    pu = pro.enter_context(nc.sbuf_tensor("pu", [128, D], F32))
    pi = pro.enter_context(nc.sbuf_tensor("pi", [128, D], I32))
    pc = pro.enter_context(nc.sbuf_tensor("pc", [128, D], BF16))
    ps_a = pro.enter_context(nc.psum_tensor("ps_a", [1, 1], F32))
    ps_b = pro.enter_context(nc.psum_tensor("ps_b", [128, 1], F32))

    for i in range(NCHUNK):
        p.dma("sp", lambda e, i=i: e.dma_start(
            out=wf[:, i * D:(i + 1) * D], in_=wffnT[i * 128:(i + 1) * 128, :]),
            writes=[("wf", i)], ring=("wf", i))
        p.op("act", lambda e, i=i: e.activation(
            pu[:], wf[:, i * D:(i + 1) * D], AF.Abs, accum_out=asums[:, i:i + 1]),
            reads=[("wf", i), "cst", "pu"], writes=[("as", i), "pu"])
    p.op("dve", lambda e: e.tensor_reduce(atmp[:], asums[:], AX.X, ALU.add),
         reads=[("as", i) for i in range(NCHUNK)], writes=["atmp"])
    p.op("pe", lambda e: e.matmul(ps_a[:], atmp[:], ones_f[:], start=True, stop=True),
         reads=["atmp", "ones_f"], writes=["ps_a"])
    p.op("dve", lambda e: e.tensor_scalar(ascl[0:1, 0:1], ps_a[:], 1.0 / (D * D),
                                          FFN_EPS, ALU.mult, ALU.max),
         reads=["ps_a"], writes=["ascl0"])
    p.op("dve", lambda e: e.reciprocal(ascl[0:1, 1:2], ascl[0:1, 0:1]),
         reads=["ascl0"], writes=["ascl1"])
    p.op("pe", lambda e: e.matmul(ps_b[:], ones_row[:], ascl[0:1, 0:1],
                                  start=True, stop=True),
         reads=["ones_row", "ascl0"], writes=["ps_b"])
    p.op("dve", lambda e: e.tensor_copy(alpha_b[:, 0:1], ps_b[:]),
         reads=["ps_b"], writes=["alv"])
    p.op("pe", lambda e: e.matmul(ps_b[:], ones_row[:], ascl[0:1, 1:2],
                                  start=True, stop=True),
         reads=["ones_row", "ascl1", "alv"], writes=["ps_b"])
    p.op("dve", lambda e: e.tensor_copy(alpha_b[:, 1:2], ps_b[:]),
         reads=["ps_b"], writes=["arv"])

    for i in range(NCHUNK):
        wfi = wf[:, i * D:(i + 1) * D]
        p.op("dve", lambda e, wfi=wfi: e.tensor_scalar(
            pu[:], wfi, alpha_b[:, 1:2], 1.25, ALU.mult, ALU.min),
            reads=[("wf", i), "arv", "pu", "pc"], writes=["pu"])
        p.op("dve", lambda e: e.tensor_scalar(pu[:], pu[:], -1.25, 0.5, ALU.max, ALU.add),
             reads=["pu"], writes=["pu"])
        p.op("dve", lambda e: e.tensor_copy(pi[:], pu[:]), reads=["pu"], writes=["pi"])
        p.op("dve", lambda e: e.tensor_tensor(pc[:], pi[:], pu[:], ALU.is_gt),
             reads=["pi", "pu"], writes=["pc"])
        p.op("dve", lambda e: e.tensor_tensor(wqz[:], pi[:], pc[:], ALU.subtract),
             reads=["pi", "pc", "wqz"], writes=["wqz"])
        p.dma("sp", lambda e, i=i: e.dma_start(
            out=s_wffnq[i * 128:(i + 1) * 128, :], in_=wqz[:]),
            reads=["wqz"], ring="sw")
    p.barrier()
    pro.close()

    def load_weight_scratch_bf16(stk, wname, dram_bf, cols):
        wsb = stk.enter_context(nc.sbuf_tensor(wname, [128, NCHUNK * cols], BF16))
        for i in range(NCHUNK):
            p.dma("sp", lambda e, i=i: e.dma_start(
                out=wsb[:, i * cols:(i + 1) * cols],
                in_=dram_bf[i * 128:(i + 1) * 128, :]),
                writes=[(wname, i)], ring=("wld", i))
        return wsb

    def load_weight_bf16(stk, wname, dram, cols, stage, stage_cols):
        wsb = stk.enter_context(nc.sbuf_tensor(wname, [128, NCHUNK * cols], BF16))
        for i in range(NCHUNK):
            slot = (i % 2) if stage_cols else 0
            sl = stage[:, slot * stage_cols:slot * stage_cols + cols]
            p.dma("sp", lambda e, sl=sl, i=i: e.dma_start(
                out=sl, in_=dram[i * 128:(i + 1) * 128, :]),
                writes=[("wstg", stage.name, slot)], ring=f"ws{slot}")
            p.op("dve", lambda e, sl=sl, i=i: e.tensor_copy(
                wsb[:, i * cols:(i + 1) * cols], sl),
                reads=[("wstg", stage.name, slot)], writes=[(wname, i)])
        return wsb

    def loadT_sup(s, dram_bf, stage, tok_st, dram_toks, ring, reads_extra=()):
        t0 = s * SUP
        for i in range(NCHUNK):
            p.dma("sp", lambda e, i=i, t0=t0: e.dma_start(
                out=stage[:, i, :],
                in_=dram_bf[t0:t0 + SUP, i * 128:(i + 1) * 128], transpose=True),
                reads=list(dram_toks) + list(reads_extra),
                writes=[tok_st + (s % 2, i)], ring=f"{ring}{s % 2}", trans=True)

    def quant_load(t, ten, sname, src_dram, ring_load):
        kk = t % (2 * tpsup)
        p.dma("sp", lambda e, t=t, kk=kk: e.dma_start(
            out=ten["x"][kk][:], in_=src_dram[t * TILE:(t + 1) * TILE, :]),
            writes=[(sname, "x", kk)], ring=f"{ring_load}{kk}")

    def quant_A(t, ten, sname, src_dram, ring_load):
        k = t % tpsup
        kk = t % (2 * tpsup)
        xbuf = ten["x"][kk]
        xtok = ten["bn"]
        pp = (t // tpsup) % 2
        tok = (sname, "x", kk)
        SA = (sname, "stA", pp)
        ub = ten["asc"]
        p.op("act", lambda e, pp=pp, k=k: e.activation(
            ub[:], xbuf[:], AF.Copy, accum_out=ten["mv"][:, pp, k, 0:1]),
            reads=[tok, "cst", (sname, "asc")],
            writes=[SA, (sname, "asc")])
        p.op("act", lambda e, pp=pp, k=k: e.activation(
            ub[:], xbuf[:], AF.Square, accum_out=ten["mv"][:, pp, k, 1:2]),
            reads=[tok, "cst", (sname, "asc")],
            writes=[SA, (sname, "asc")])
        p.op("dve", lambda e, pp=pp, k=k: e.tensor_reduce(
            ten["mx"][:, pp, k:k + 1], xbuf[:], AX.X, ALU.max),
            reads=[tok, SA], writes=[SA])
        p.op("dve", lambda e, pp=pp, k=k: e.tensor_reduce(
            ten["mn"][:, pp, k:k + 1], xbuf[:], AX.X, ALU.min),
            reads=[tok, SA], writes=[SA])

    def quant_B(s, ten, sQ, sname):
        pp = s % 2
        SA = (sname, "stA", pp)
        SB = (sname, "stB", pp)
        mv, mx, mn = ten["mv"], ten["mx"], ten["mn"]
        bt, qs, qb = ten["beta"], ten["qs"], ten["qb"]
        mean = mv[:, pp, :, 0:1]
        var = mv[:, pp, :, 1:2]
        meanf = mean.rearrange("p a b -> p (a b)")
        varf = var.rearrange("p a b -> p (a b)")
        m2 = ten["m2"][:, pp, :]
        p.op("dve", lambda e: e.tensor_scalar(meanf, meanf, 1.0 / D, None, ALU.mult),
             reads=[SA, SB], writes=[SB])
        p.op("dve", lambda e: e.tensor_scalar(varf, varf, 1.0 / D, None, ALU.mult),
             reads=[SA, SB], writes=[SB])
        p.op("dve", lambda e: e.tensor_tensor(m2, meanf, meanf, ALU.mult),
             reads=[SB], writes=[SB])
        p.op("dve", lambda e: e.tensor_tensor(varf, varf, m2, ALU.subtract),
             reads=[SB], writes=[SB])
        p.op("dve", lambda e: e.tensor_scalar(var, var, LN_EPS, None, ALU.add),
             reads=[SB], writes=[SB])
        p.op("act", lambda e: e.activation(var, var, AF.Sqrt),
             reads=[SB, "cst"], writes=[SB])
        p.op("dve", lambda e: e.reciprocal(var, var), reads=[SB], writes=[SB])
        p.op("dve", lambda e: e.tensor_tensor(mx[:, pp, :], mx[:, pp, :], meanf,
                                              ALU.subtract),
             reads=[SA, SB], writes=[SB])
        p.op("dve", lambda e: e.tensor_tensor(mn[:, pp, :], meanf, mn[:, pp, :],
                                              ALU.subtract),
             reads=[SA, SB], writes=[SB])
        p.op("dve", lambda e: e.tensor_tensor(mx[:, pp, :], mx[:, pp, :], mn[:, pp, :],
                                              ALU.max),
             reads=[SB], writes=[SB])
        p.op("dve", lambda e: e.tensor_tensor(bt[:, pp, :], mx[:, pp, :], varf, ALU.mult),
             reads=[SB], writes=[SB])
        p.op("dve", lambda e: e.tensor_scalar(bt[:, pp, :], bt[:, pp, :],
                                              1.0 / 127.0, 1e-8, ALU.mult, ALU.max),
             reads=[SB], writes=[SB])
        p.op("dve", lambda e: e.reciprocal(qs[:, pp, :], bt[:, pp, :]),
             reads=[SB], writes=[SB])
        p.op("dve", lambda e: e.tensor_tensor(qs[:, pp, :], qs[:, pp, :], varf, ALU.mult),
             reads=[SB], writes=[SB])
        p.op("dve", lambda e: e.tensor_tensor(qb[:, pp, :], meanf, qs[:, pp, :], ALU.mult),
             reads=[SB], writes=[SB])
        p.op("dve", lambda e: e.tensor_scalar(qb[:, pp, :], qb[:, pp, :],
                                              -1.0, 0.5, ALU.mult, ALU.add),
             reads=[SB], writes=[SB])
        p.op("dve", lambda e, s=s: e.tensor_scalar(
            sQ[:, s * tpsup:(s + 1) * tpsup], bt[:, pp, :], alpha_b[:, 0:1], 0.5,
            ALU.mult, ALU.mult),
            reads=[SB, "alv"], writes=[(sname, "sQ", s)])

    def quant_C(t, ten, sname):
        k = t % tpsup
        kk = t % (2 * tpsup)
        j = t % 2
        xbuf = ten["x"][kk]
        ub, ib, cb, xqb = ten["u"][j], ten["i"], ten["c"], ten["xq"][t % 4]
        pp = (t // tpsup) % 2
        SB = (sname, "stB", pp)
        tok = (sname, "x", kk)
        p.op("act", lambda e, pp=pp, k=k: e.activation(
            ub[:], xbuf[:], AF.Identity,
            bias=ten["qb"][:, pp, k:k + 1], scale=ten["qs"][:, pp, k:k + 1]),
            reads=[tok, SB, "cst", (sname, "u", j)], writes=[(sname, "u", j)])
        p.op("dve", lambda e: e.tensor_copy(ib[:], ub[:]),
             reads=[(sname, "u", j)], writes=[(sname, "i")])
        p.op("dve", lambda e: e.tensor_tensor(cb[:], ib[:], ub[:], ALU.is_gt),
             reads=[(sname, "i"), (sname, "u", j)], writes=[(sname, "c")])
        p.op("dve", lambda e: e.tensor_tensor(xqb[:], ib[:], cb[:], ALU.subtract),
             reads=[(sname, "i"), (sname, "c"), (sname, "xqb", t % 4)],
             writes=[(sname, "xqb", t % 4)])

    def ffn_local(t, ten, xqT, ptb, wname, wsb, psum, x1b, sQ, sname,
                  dst_dram, ring_out, accsum=None):
        k = t % tpsup
        kk = t % (2 * tpsup)
        j = t % 2
        xqb = ten["xq"][t % 4]
        XT = (sname, "xqT")
        for g in range(2):
            for i2 in range(8):
                i = g * 8 + i2
                p.op("pe", lambda e, i=i, i2=i2: e.transpose(
                    ptb[:, i2 * 128:(i2 + 1) * 128], xqb[:, i * 128:(i + 1) * 128], idb[:]),
                    reads=[(sname, "xqb", t % 4), "idb"],
                    writes=[(sname, "pt", i2)])
            p.op("act", lambda e, g=g: e.activation(
                xqT[:, g * 8:(g + 1) * 8, :].rearrange("p a b -> p (a b)"), ptb[:],
                AF.Copy),
                reads=[(sname, "pt", i2) for i2 in range(8)] + [XT, "cst"],
                writes=[XT] + [(sname, "pt", i2) for i2 in range(8)])
        for i in range(NCHUNK):
            for oc in range(4):
                p.op("pe", lambda e, i=i, oc=oc: e.matmul(
                    psum[:, oc * 512:(oc + 1) * 512],
                    xqT[:, i, :],
                    wsb[:, i * D + oc * 512: i * D + oc * 512 + 512],
                    start=(i == 0), stop=(i == NCHUNK - 1)),
                    reads=[XT, (wname, i)],
                    writes=[(sname, "ps")])
        kw = {}
        wl = [(sname, "x1", t % 2)]
        if accsum is not None:
            kw["accum_out"] = accsum
            wl.append((sname, "lnA"))
        p.op("dve", lambda e, t=t, kk=kk, kw=kw: e.scalar_tensor_tensor(
            out=x1b[:], in0=psum[:], scalar=sQ[:, t:t + 1], in1=ten["x"][kk][:],
            op0=ALU.mult, op1=ALU.add, **kw),
            reads=[(sname, "ps"), (sname, "sQ", t // tpsup), (sname, "x", kk)],
            writes=wl)
        p.dma("sp", lambda e, t=t: e.dma_start(
            out=dst_dram[t * TILE:(t + 1) * TILE, :], in_=x1b[:]),
            reads=[(sname, "x1", t % 2)], writes=[("dram", sname, "o", t)],
            ring=f"{ring_out}{t % 2}")

    def ffn_back(t, s, stage, wname, wsb, psum, xre, x1b, sQ, sname,
                 src_dram, dst_dram, ring_re, ring_out, accsum=None):
        toff = (t % tpsup) * TILE
        p.dma("sp", lambda e, t=t: e.dma_start(
            out=xre[:], in_=src_dram[t * TILE:(t + 1) * TILE, :]),
            writes=[(sname, "xre", t % 2)], ring=f"{ring_re}{t % 2}")
        for i in range(NCHUNK):
            for oc in range(4):
                p.op("pe", lambda e, i=i, oc=oc, toff=toff: e.matmul(
                    psum[:, oc * 512:(oc + 1) * 512],
                    stage[:, i, toff:toff + TILE],
                    wsb[:, i * D + oc * 512: i * D + oc * 512 + 512],
                    start=(i == 0), stop=(i == NCHUNK - 1)),
                    reads=[(sname, "st", s % 2, i), (wname, i)],
                    writes=[(sname, "ps", t % 2)])
        kw = {}
        wl = [(sname, "x1", t % 2)]
        if accsum is not None:
            kw["accum_out"] = accsum
            wl.append((sname, "lnA"))
        p.op("dve", lambda e, t=t, kw=kw: e.scalar_tensor_tensor(
            out=x1b[:], in0=psum[:], scalar=sQ[:, t:t + 1], in1=xre[:],
            op0=ALU.mult, op1=ALU.add, **kw),
            reads=[(sname, "ps", t % 2), (sname, "sQ", t // tpsup), (sname, "xre", t % 2)],
            writes=wl)
        p.dma("sp", lambda e, t=t: e.dma_start(
            out=dst_dram[t * TILE:(t + 1) * TILE, :], in_=x1b[:]),
            reads=[(sname, "x1", t % 2)], writes=[("dram", sname, "o", t)],
            ring=f"{ring_out}{t % 2}")

    if "1" not in stages:
        p.wait_all("sp")
        p.emit(nc, root)
        root.close()
        nc.finalize()
        return nc
    s1 = ExitStack()
    wffn_sb = load_weight_scratch_bf16(s1, "w_s1", s_wffnq, D)

    def mk_front_tensors(stk, pre):
        return {
            "x": [stk.enter_context(nc.sbuf_tensor(f"{pre}x{j}", [128, D], F32))
                  for j in range(2 * tpsup)],
            "bn": stk.enter_context(nc.sbuf_tensor(f"{pre}bn", [128, 24], F32)),
            "u": [stk.enter_context(nc.sbuf_tensor(f"{pre}u{j}", [128, D], F32))
                  for j in range(2)],
            "i": stk.enter_context(nc.sbuf_tensor(f"{pre}i", [128, D], I32)),
            "c": stk.enter_context(nc.sbuf_tensor(f"{pre}c", [128, D], BF16)),
            "xq": [stk.enter_context(nc.sbuf_tensor(f"{pre}xq{j}", [128, D], BF16))
                   for j in range(4)],
            "mv": stk.enter_context(nc.sbuf_tensor(f"{pre}mv", [128, 2, tpsup, 2], F32)),
            "m2": stk.enter_context(nc.sbuf_tensor(f"{pre}m2", [128, 2, tpsup], F32)),
            "mx": stk.enter_context(nc.sbuf_tensor(f"{pre}mx", [128, 2, tpsup], F32)),
            "mn": stk.enter_context(nc.sbuf_tensor(f"{pre}mn", [128, 2, tpsup], F32)),
            "beta": stk.enter_context(nc.sbuf_tensor(f"{pre}bt", [128, 2, tpsup], F32)),
            "qs": stk.enter_context(nc.sbuf_tensor(f"{pre}qs", [128, 2, tpsup], F32)),
            "qb": stk.enter_context(nc.sbuf_tensor(f"{pre}qb", [128, 2, tpsup], F32)),
            "asc": stk.enter_context(nc.sbuf_tensor(f"{pre}asc", [128, D], BF16)),
        }

    ten1 = mk_front_tensors(s1, "s1")
    xqT1 = s1.enter_context(nc.sbuf_tensor("s1xqT", [128, NCHUNK, 128], BF16))
    ptb1 = s1.enter_context(nc.psum_tensor("s1ptb", [128, 1024], BF16))
    x1b1 = [s1.enter_context(nc.sbuf_tensor(f"s1o{j}", [128, D], F32)) for j in range(2)]
    lnsum = s1.enter_context(nc.sbuf_tensor("s1lsu", [128, 2], F32))
    lnss = s1.enter_context(nc.sbuf_tensor("s1lss", [128, 2], F32))
    lnmn = s1.enter_context(nc.sbuf_tensor("s1lmn", [128, 2], F32))
    lnm2 = s1.enter_context(nc.sbuf_tensor("s1lm2", [128, 2], F32))
    lnrs = s1.enter_context(nc.sbuf_tensor("s1lrs", [128, 2], F32))
    lnnb = s1.enter_context(nc.sbuf_tensor("s1lnb", [128, 2], F32))
    xn2b = s1.enter_context(nc.sbuf_tensor("s1xn", [128, D], BF16))
    sqrb = ten1["u"][0]
    psum1 = s1.enter_context(nc.psum_tensor("s1p", [128, D], F32))

    def s1_ln2_pair(t0):
        LA, LB = ("s1", "lnA"), ("s1", "lnB")
        for t in (t0, t0 + 1):
            j = t % 2
            p.op("act", lambda e, j=j: e.activation(
                sqrb[:], x1b1[j][:], AF.Square, accum_out=lnss[:, j:j + 1]),
                reads=[("s1", "x1", j), "cst", ("s1", "u", 0), LA],
                writes=[LA, ("s1", "u", 0)])
        p.op("dve", lambda e: e.tensor_scalar(lnmn[:], lnsum[:], 1.0 / D, None, ALU.mult),
             reads=[LA, LB], writes=[LB])
        p.op("dve", lambda e: e.tensor_scalar(lnss[:], lnss[:], 1.0 / D, None, ALU.mult),
             reads=[LA, LB], writes=[LB])
        p.op("dve", lambda e: e.tensor_tensor(lnm2[:], lnmn[:], lnmn[:], ALU.mult),
             reads=[LB], writes=[LB])
        p.op("dve", lambda e: e.tensor_tensor(lnrs[:], lnss[:], lnm2[:], ALU.subtract),
             reads=[LB], writes=[LB])
        p.op("dve", lambda e: e.tensor_scalar(lnrs[:], lnrs[:], LN_EPS, None, ALU.add),
             reads=[LB], writes=[LB])
        p.op("act", lambda e: e.activation(lnrs[:], lnrs[:], AF.Sqrt),
             reads=[LB, "cst"], writes=[LB])
        p.op("dve", lambda e: e.reciprocal(lnrs[:], lnrs[:]), reads=[LB], writes=[LB])
        p.op("dve", lambda e: e.tensor_tensor(lnnb[:], lnmn[:], lnrs[:], ALU.mult),
             reads=[LB], writes=[LB])
        p.op("dve", lambda e: e.tensor_scalar(lnnb[:], lnnb[:], -1.0, None, ALU.mult),
             reads=[LB], writes=[LB])
        for t in (t0, t0 + 1):
            j = t % 2
            p.op("act", lambda e, j=j: e.activation(
                xn2b[:], x1b1[j][:], AF.Identity, bias=lnnb[:, j:j + 1],
                scale=lnrs[:, j:j + 1]),
                reads=[("s1", "x1", j), LB, "cst", ("s1", "xn2")],
                writes=[("s1", "xn2")])
            p.dma("sp", lambda e, t=t: e.dma_start(
                out=s_xn2[t * TILE:(t + 1) * TILE, :], in_=xn2b[:]),
                reads=[("s1", "xn2")], writes=[("dram", "xn2", t)], ring="sxn")

    for t in range(0, 2 * tpsup):
        quant_load(t, ten1, "s1", x_in, "lx")
    for t in range(0, tpsup):
        quant_A(t, ten1, "s1", x_in, "lx")
    quant_B(0, ten1, sQ1, "s1")
    pend1 = []
    for s in range(nSup):
        for t0 in range(s * tpsup, (s + 1) * tpsup, 2):
            quant_C(t0, ten1, "s1")
            quant_C(t0 + 1, ten1, "s1")
            if pend1:
                pt0 = pend1.pop(0)
                for t in (pt0, pt0 + 1):
                    ffn_local(t, ten1, xqT1, ptb1, "w_s1", wffn_sb, psum1,
                              x1b1[t % 2], sQ1, "s1", s_x1, "sx1",
                              accsum=lnsum[:, t % 2:t % 2 + 1])
                s1_ln2_pair(pt0)
                for t in (pt0 + 2 * tpsup, pt0 + 2 * tpsup + 1):
                    if t < nT:
                        quant_load(t, ten1, "s1", x_in, "lx")
            if s + 1 < nSup:
                for t in (t0 + tpsup, t0 + tpsup + 1):
                    quant_A(t, ten1, "s1", x_in, "lx")
            pend1.append(t0)
        if s + 1 < nSup:
            quant_B(s + 1, ten1, sQ1, "s1")
    while pend1:
        pt0 = pend1.pop(0)
        for t in (pt0, pt0 + 1):
            ffn_local(t, ten1, xqT1, ptb1, "w_s1", wffn_sb, psum1,
                      x1b1[t % 2], sQ1, "s1", s_x1, "sx1",
                      accsum=lnsum[:, t % 2:t % 2 + 1])
        s1_ln2_pair(pt0)
    p.barrier()
    s1.close()

    if "3" not in stages:
        p.wait_all("sp")
        p.emit(nc, root)
        root.close()
        nc.finalize()
        return nc
    s3 = ExitStack()
    bqb = s3.enter_context(nc.sbuf_tensor("bqb", [128, D], F32))
    bkb = s3.enter_context(nc.sbuf_tensor("bkb", [128, DK], F32))
    bvb = s3.enter_context(nc.sbuf_tensor("bvb", [128, DK], F32))
    p.dma("sp", lambda e: e.dma_start(out=bqb[:], in_=bq.partition_broadcast(128)),
          writes=["bqb"], ring="bq")
    p.dma("sp", lambda e: e.dma_start(out=bkb[:], in_=bk.partition_broadcast(128)),
          writes=["bkb"], ring="bk")
    p.dma("sp", lambda e: e.dma_start(out=bvb[:], in_=bv.partition_broadcast(128)),
          writes=["bvb"], ring="bv")
    wstage = s3.enter_context(nc.sbuf_tensor("wstg", [128, 2 * D], F32))
    wq_sb = load_weight_bf16(s3, "wq", wqT, D, wstage, D)
    wk_sb = load_weight_bf16(s3, "wk", wkT, DK, wstage, D)
    wv_sb = load_weight_bf16(s3, "wv", wvT, DK, wstage, D)
    stage3 = [s3.enter_context(nc.sbuf_tensor(f"s3tg{j}", [128, NCHUNK, SUP], BF16))
              for j in range(2)]
    qb_ = s3.enter_context(nc.sbuf_tensor("s3q", [128, D], F32))
    mb_ = s3.enter_context(nc.sbuf_tensor("s3m", [128, D], F32))
    eb_ = s3.enter_context(nc.sbuf_tensor("s3e", [128, D], F32))
    sqb_ = s3.enter_context(nc.sbuf_tensor("s3sq", [128, D], BF16))
    kfb = s3.enter_context(nc.sbuf_tensor("s3kf", [128, DK], F32))
    kmb = s3.enter_context(nc.sbuf_tensor("s3km", [128, DK], F32))
    keb = s3.enter_context(nc.sbuf_tensor("s3ke", [128, DK], F32))
    skb = [s3.enter_context(nc.sbuf_tensor(f"s3sk{j}", [128, DK], BF16))
           for j in range(2)]
    vbf = [s3.enter_context(nc.sbuf_tensor(f"s3v{j}", [128, KVH, 132], BF16))
           for j in range(2)]
    psq = s3.enter_context(nc.psum_tensor("s3pq", [128, D], F32))
    psk = s3.enter_context(nc.psum_tensor("s3pk", [128, DK], F32))
    psv = s3.enter_context(nc.psum_tensor("s3pv", [128, DK], F32))
    pskv = [s3.enter_context(nc.psum_tensor(f"s3pkv{j}", [128, 2, 132], F32))
            for j in range(2)]
    kv_acc = s3.enter_context(nc.sbuf_tensor("kv_acc", [128, nSeq, KVH, 132], F32))

    p.op("dve", lambda e: e.memset(kv_acc[:], 0.0), writes=[("s3", "kva")])
    for j in range(2):
        p.op("dve", lambda e, j=j: e.memset(vbf[j][:], 0.0), writes=[("s3", "v", j)])
        p.op("dve", lambda e, j=j: e.memset(vbf[j][:, :, 128:129], 1.0),
             writes=[("s3", "v", j)])
    wtoks3 = ([("wq", i) for i in range(NCHUNK)] + [("wk", i) for i in range(NCHUNK)]
              + [("wv", i) for i in range(NCHUNK)])
    kv_pend = [None]
    sup_list = list(range(nSup))
    loadT_sup(0, s_xn2, stage3[0], ("s3", "st"), [], "txn", reads_extra=wtoks3)
    for gsup in sup_list:
        if gsup + 1 < nSup:
            loadT_sup(gsup + 1, s_xn2, stage3[(gsup + 1) % 2], ("s3", "st"), [], "txn",
                      reads_extra=(wtoks3 if gsup + 1 < 2 else []))
        sq_ = (gsup * SUP) // SEQ
        for u_loc in range(tpsup):
            t = gsup * tpsup + u_loc
            u = t % tps
            toff = u_loc * TILE
            stg = stage3[gsup % 2]
            for i in range(NCHUNK):
                for oc in range(4):
                    p.op("pe", lambda e, i=i, oc=oc, toff=toff, stg=stg: e.matmul(
                        psq[:, oc * 512:(oc + 1) * 512], stg[:, i, toff:toff + TILE],
                        wq_sb[:, i * D + oc * 512:i * D + oc * 512 + 512],
                        start=(i == 0), stop=(i == NCHUNK - 1)),
                        reads=[("s3", "st", gsup % 2, i), ("wq", i)],
                        writes=[("s3", "psq")])
                p.op("pe", lambda e, i=i, toff=toff, stg=stg: e.matmul(
                    psk[:], stg[:, i, toff:toff + TILE], wk_sb[:, i * DK:(i + 1) * DK],
                    start=(i == 0), stop=(i == NCHUNK - 1)),
                    reads=[("s3", "st", gsup % 2, i), ("wk", i)], writes=[("s3", "psk")])
                p.op("pe", lambda e, i=i, toff=toff, stg=stg: e.matmul(
                    psv[:], stg[:, i, toff:toff + TILE], wv_sb[:, i * DK:(i + 1) * DK],
                    start=(i == 0), stop=(i == NCHUNK - 1)),
                    reads=[("s3", "st", gsup % 2, i), ("wv", i)], writes=[("s3", "psv")])
            if kv_pend[0] is not None:
                kv_pend[0]()
                kv_pend[0] = None
            p.op("dve", lambda e: e.tensor_tensor(qb_[:], psq[:], bqb[:], ALU.add),
                 reads=[("s3", "psq"), "bqb", ("s3", "qf")], writes=[("s3", "qf")])
            p.op("act", lambda e: e.activation(eb_[:], qb_[:], AF.Exp),
                 reads=[("s3", "qf"), "cst", ("s3", "qe")], writes=[("s3", "qe")])
            p.op("dve", lambda e: e.tensor_scalar(mb_[:], eb_[:], 1.0, None, ALU.min),
                 reads=[("s3", "qe"), ("s3", "qm")], writes=[("s3", "qm")])
            p.op("dve", lambda e: e.scalar_tensor_tensor(
                out=sqb_[:], in0=qb_[:], scalar=0.0, in1=mb_[:], op0=ALU.max, op1=ALU.add),
                reads=[("s3", "qf"), ("s3", "qm"), ("s3", "sq")], writes=[("s3", "sq")])
            p.dma("sp", lambda e, t=t: e.dma_start(
                out=s_sq[t * TILE:(t + 1) * TILE, :], in_=sqb_[:]),
                reads=[("s3", "sq")], writes=[("dram", "sq", t)], ring="ssq")
            p.op("dve", lambda e: e.tensor_tensor(kfb[:], psk[:], bkb[:], ALU.add),
                 reads=[("s3", "psk"), "bkb", ("s3", "kf")], writes=[("s3", "kf")])
            p.op("act", lambda e: e.activation(keb[:], kfb[:], AF.Exp),
                 reads=[("s3", "kf"), "cst", ("s3", "ke")], writes=[("s3", "ke")])
            p.op("dve", lambda e: e.tensor_scalar(kmb[:], keb[:], 1.0, None, ALU.min),
                 reads=[("s3", "ke"), ("s3", "km")], writes=[("s3", "km")])
            j2 = t % 2
            p.op("dve", lambda e, j2=j2: e.scalar_tensor_tensor(
                out=skb[j2][:], in0=kfb[:], scalar=0.0, in1=kmb[:],
                op0=ALU.max, op1=ALU.add),
                reads=[("s3", "kf"), ("s3", "km"), ("s3", "sk", j2)],
                writes=[("s3", "sk", j2)])
            p.op("dve", lambda e, j2=j2: e.tensor_tensor(
                vbf[j2][:, :, 0:128], psv[:].rearrange("p (j e) -> p j e", j=KVH),
                bvb[:].rearrange("p (j e) -> p j e", j=KVH), ALU.add),
                reads=[("s3", "psv"), "bvb", ("s3", "v", j2)],
                writes=[("s3", "v", j2)])

            def kv_emit(j2=j2, sq_=sq_):
                for j in range(KVH):
                    p.op("pe", lambda e, j=j, j2=j2: e.matmul(
                        pskv[j // 2][:, j % 2, 0:132], skb[j2][:, j * 128:(j + 1) * 128],
                        vbf[j2][:, j, 0:132], start=True, stop=True),
                        reads=[("s3", "sk", j2), ("s3", "v", j2)],
                        writes=[("s3", "kv", j // 2)])
                for bk_ in range(2):
                    p.op("dve", lambda e, sq_=sq_, bk_=bk_: e.tensor_tensor(
                        kv_acc[:, sq_, 2 * bk_:2 * bk_ + 2, :], pskv[bk_][:],
                        kv_acc[:, sq_, 2 * bk_:2 * bk_ + 2, :], ALU.add),
                        reads=[("s3", "kv", bk_), ("s3", "kva")], writes=[("s3", "kva")])

            if u == tps - 1:
                kv_emit()
                kv_pend[0] = None
            else:
                kv_pend[0] = kv_emit
            if u == tps - 1:
                p.op("dve", lambda e, sq_=sq_: e.tensor_copy(
                    kv_sb[:, sq_ * DK:(sq_ + 1) * DK].rearrange("p (j e) -> p j e", j=KVH),
                    kv_acc[:, sq_, :, 0:128]),
                    reads=[("s3", "kva")], writes=[("kvsb", sq_)])
                p.op("dve", lambda e, sq_=sq_: e.tensor_copy(
                    ks_cols[:, sq_, :], kv_acc[:, sq_, :, 128:129].rearrange(
                        "p j o -> p (j o)")),
                    reads=[("s3", "kva")], writes=[("kssb", sq_)])
    if dbg:
        kvacc_dbg = nc.dram_tensor("kvacc_dbg", [128, nSeq * KVH * 132], F32, kind=knd).ap()
        p.dma("sp", lambda e: e.dma_start(out=kvacc_dbg, in_=kv_acc[:].rearrange("p a b c -> p (a b c)")),
              reads=[("s3", "kva")], ring="dbg1")
    p.barrier()
    s3.close()

    S4WO = "y" not in stages
    S4SEQPRO = "u" not in stages
    if "4" not in stages:
        p.wait_all("sp")
        p.emit(nc, root)
        root.close()
        nc.finalize()
        return nc
    s4 = ExitStack()
    wstage4 = s4.enter_context(nc.sbuf_tensor("wstg4", [128, D], F32))
    if S4WO:
        wo_sb = load_weight_bf16(s4, "w_s4", woT, D, wstage4, 0)
    else:
        wo_sb = s4.enter_context(nc.sbuf_tensor("w_s4", [128, NCHUNK * D], BF16))
    sqT = [s4.enter_context(nc.sbuf_tensor(f"s4qT{j}", [128, NCHUNK, TILE], BF16))
           for j in range(2)]
    stage4a = [s4.enter_context(nc.sbuf_tensor(f"s4ta{j}", [128, NCHUNK, SUP], BF16))
               for j in range(2)]
    rnb = s4.enter_context(nc.sbuf_tensor("s4rn", [128, H], F32))
    attnb = [s4.enter_context(nc.sbuf_tensor(f"s4at{j}", [128, D], BF16))
             for j in range(2)]
    x1l = [s4.enter_context(nc.sbuf_tensor(f"s4x1{j}", [128, D], F32)) for j in range(2)]
    x2b = [s4.enter_context(nc.sbuf_tensor(f"s4x2{j}", [128, D], F32)) for j in range(2)]
    psb = s4.enter_context(nc.psum_tensor("s4pb", [128, 512], F32))
    ksb_b = s4.enter_context(nc.sbuf_tensor("s4ksb", [128, D], F32))
    ksrow = s4.enter_context(nc.sbuf_tensor("s4ksr", [1, KVH, 128], F32))
    sqtm = [s4.enter_context(nc.sbuf_tensor(f"s4sqt{j}", [128, D], BF16)) for j in range(2)]
    nmt = s4.enter_context(nc.sbuf_tensor("s4nmt", [128, D], F32))
    nrm = s4.enter_context(nc.sbuf_tensor("s4nrm", [128, H], F32))
    stage_dbg = nc.dram_tensor("stage_dbg", [128, NCHUNK * SUP], BF16, kind=knd).ap()
    norm_dbg = nc.dram_tensor("norm_dbg", [NT, H], F32, kind=knd).ap()

    def s4_seqpro(sq_):
        for j in range(KVH):
            p.op("pe", lambda e, j=j, sq_=sq_: e.transpose(
                psb[0:1, j * 128:(j + 1) * 128], ks_cols[:, sq_, j:j + 1], idf[:]),
                reads=[("kssb", sq_), "idf", ("s4", "ksr")], writes=[("s4", "psbT", j)])
        p.op("dve", lambda e: e.tensor_copy(
            ksrow[0:1, :, :], psb[0:1, 0:512].rearrange("p (j d) -> p j d", j=KVH)),
            reads=[("s4", "psbT", 0), ("s4", "psbT", 1), ("s4", "psbT", 2),
                   ("s4", "psbT", 3)], writes=[("s4", "ksr")])
        for grp in range(KVH):
            for hh in range(KVH):
                j = (grp * KVH + hh) // GROUPS
                p.op("pe", lambda e, j=j, hh=hh: e.matmul(
                    psb[:, hh * 128:(hh + 1) * 128], ones_row[:],
                    ksrow[0:1, j, :], start=True, stop=True),
                    reads=[("s4", "ksr"), "ones_row", ("s4", "ksbb")],
                    writes=[("s4", "psbB", hh)])
            p.op("act", lambda e, grp=grp: e.activation(
                ksb_b[:, grp * 512:(grp + 1) * 512], psb[:], AF.Copy),
                reads=[("s4", "psbB", 0), ("s4", "psbB", 1), ("s4", "psbB", 2),
                       ("s4", "psbB", 3), "cst"], writes=[("s4", "ksbb")])
    psqk = [s4.enter_context(nc.psum_tensor(f"s4pq{j}", [128, 512], F32)) for j in range(2)]
    psout = s4.enter_context(nc.psum_tensor("s4po", [128, D], F32))
    ptb4 = s4.enter_context(nc.psum_tensor("s4ptb", [128, 1024], BF16))

    def s4_sqload(t):
        p.dma("sp", lambda e, t=t: e.dma_start(
            out=sqtm[t % 2][:], in_=s_sq[t * TILE:(t + 1) * TILE, :]),
            writes=[("s4", "sqtm", t % 2)], ring=f"lsq{t % 2}")

    def s4_x1load(t):
        p.dma("sp", lambda e, t=t: e.dma_start(
            out=x1l[t % 2][:], in_=s_x1[t * TILE:(t + 1) * TILE, :]),
            writes=[("s4", "x1l", t % 2)], ring=f"lx1{t % 2}")

    def s4_front(t, gsup, only_mm=False, norm_only=False):
        toff = (t % tpsup) * TILE
        sq_ = t // tps
        sqt = sqT[t % 2]
        for rnd in range(2):
            for i2 in range(8):
                i = rnd * 8 + i2
                p.op("pe", lambda e, i=i, i2=i2, t=t: e.transpose(
                    ptb4[:, i2 * 128:(i2 + 1) * 128],
                    sqtm[t % 2][:, i * 128:(i + 1) * 128], idb[:]),
                    reads=[("s4", "sqtm", t % 2), "idb", ("s4", "pt4", i2)],
                    writes=[("s4", "pt4", i2)])
            p.op("act", lambda e, rnd=rnd, sqt=sqt: e.activation(
                sqt[:, rnd * 8:(rnd + 1) * 8, :],
                ptb4[:].rearrange("p (c t) -> p c t", c=8), AF.Copy),
                reads=[("s4", "pt4", i2) for i2 in range(8)] + ["cst"],
                writes=[("s4", "sqT", t % 2, he) for he in range(rnd * 8, (rnd + 1) * 8)]
                      + [("s4", "pt4", i2) for i2 in range(8)])
        p.op("dve", lambda e, t=t: e.tensor_tensor(
            nmt[:], sqtm[t % 2][:], ksb_b[:], ALU.mult),
            reads=[("s4", "sqtm", t % 2), ("s4", "ksbb"), ("s4", "nmt")],
            writes=[("s4", "nmt")])
        p.op("dve", lambda e: e.tensor_reduce(
            nrm[:], nmt[:].rearrange("p (h d) -> p h d", h=H), AX.X, ALU.add),
            reads=[("s4", "nmt"), ("s4", "nrm")], writes=[("s4", "nrm")])
        if dbg:
            p.dma("sp", lambda e, t=t: e.dma_start(
                out=norm_dbg[t * TILE:(t + 1) * TILE, :], in_=nrm[:]),
                reads=[("s4", "nrm")], ring="dbg3")
        p.op("dve", lambda e: e.tensor_scalar(nrm[:], nrm[:], ATTN_EPS, None, ALU.add),
             reads=[("s4", "nrm")], writes=[("s4", "nrm")])
        p.op("dve", lambda e: e.reciprocal(rnb[:], nrm[:]),
             reads=[("s4", "nrm"), ("s4", "rn")], writes=[("s4", "rn")])
        if norm_only:
            return
        for h in range(H):
            j = h // GROUPS
            bk_ = h % 2
            p.op("pe", lambda e, bk_=bk_, h=h, j=j, sqt=sqt, sq_=sq_: e.matmul(
                psqk[bk_][:, 0:128], sqt[:, h, :],
                kv_sb[:, sq_ * DK + j * 128:sq_ * DK + j * 128 + 128],
                start=True, stop=True),
                reads=[("s4", "sqT", t % 2, h), ("kvsb", sq_)],
                writes=[("s4", "qk", bk_)])
            if bk_ == 0:
                p.op("dve", lambda e, h=h, bk_=bk_, t=t: e.tensor_scalar(
                    attnb[t % 2][:, h * 128:(h + 1) * 128],
                    psqk[bk_][:, 0:128],
                    rnb[:, h:h + 1], None, ALU.mult),
                    reads=[("s4", "qk", bk_), ("s4", "rn")],
                    writes=[("s4", "at", t % 2, h)])
            else:
                p.op("act", lambda e, h=h, bk_=bk_, t=t: e.activation(
                    attnb[t % 2][:, h * 128:(h + 1) * 128],
                    psqk[bk_][:, 0:128],
                    AF.Identity, scale=rnb[:, h:h + 1]),
                    reads=[("s4", "qk", bk_), ("s4", "rn"), "cst"],
                    writes=[("s4", "at", t % 2, h)])
        if dbg:
            p.dma("sp", lambda e, t=t: e.dma_start(
                out=s_attn[t * TILE:(t + 1) * TILE, :], in_=attnb[t % 2][:]),
                reads=[("s4", "at", t % 2, h2) for h2 in range(H)],
                writes=[("dram", "at", t)], ring="sat")
        stga = stage4a[gsup % 2]
        for rnd in range(2):
            for i2 in range(8):
                i = rnd * 8 + i2
                p.op("pe", lambda e, i=i, i2=i2, t=t: e.transpose(
                    ptb4[:, i2 * 128:(i2 + 1) * 128],
                    attnb[t % 2][:, i * 128:(i + 1) * 128], idb[:]),
                    reads=[("s4", "at", t % 2, i), "idb", ("s4", "pt4", i2)],
                    writes=[("s4", "pt4", i2)])
            p.op("act", lambda e, rnd=rnd, toff=toff, stga=stga: e.activation(
                stga[:, rnd * 8:(rnd + 1) * 8, toff:toff + TILE],
                ptb4[:].rearrange("p (c t) -> p c t", c=8), AF.Copy),
                reads=[("s4", "pt4", i2) for i2 in range(8)] + ["cst"],
                writes=[("s4", "ast", gsup % 2, he) for he in range(rnd * 8, (rnd + 1) * 8)]
                      + [("s4", "pt4", i2) for i2 in range(8)])

    def s4_back(t, sb_):
        toff = (t % tpsup) * TILE
        stg = stage4a[sb_ % 2]
        for he in range(NCHUNK):
            for oc in range(4):
                p.op("pe", lambda e, he=he, oc=oc, toff=toff,
                     stg=stg: e.matmul(
                    psout[:, oc * 512:(oc + 1) * 512],
                    stg[:, he, toff:toff + TILE],
                    wo_sb[:, he * D + oc * 512:he * D + oc * 512 + 512],
                    start=(he == 0), stop=(he == NCHUNK - 1)),
                    reads=[("s4", "ast", sb_ % 2, he), ("w_s4", he)],
                    writes=[("s4", "po")])
        p.op("dve", lambda e, t=t: e.tensor_tensor(
            x2b[t % 2][:], psout[:], x1l[t % 2][:], ALU.add),
            reads=[("s4", "po"), ("s4", "x1l", t % 2)],
            writes=[("s4", "x2", t % 2, 0)])
        p.dma("sp", lambda e, t=t: e.dma_start(
            out=s_x2[t * TILE:(t + 1) * TILE, :], in_=x2b[t % 2][:]),
            reads=[("s4", "x2", t % 2, 0)],
            writes=[("dram", "x2", t)], ring=f"sx2{t % 2}")

    do_loadT = "6" not in stages
    do_back = "7" not in stages
    only_wo = "q" in stages
    only_sqT = "w" in stages
    only_mm = "e" in stages
    wtoks4 = [("w_s4", i) for i in range(NCHUNK)]
    if not only_wo:
        seq_done = set()
        for gsup in range(nSup + 1):
            if gsup < nSup and S4SEQPRO:
                sq_now = (gsup * SUP) // SEQ
                if sq_now not in seq_done:
                    seq_done.add(sq_now)
                    s4_seqpro(sq_now)
            if gsup < nSup and not only_sqT:
                for t0 in range(gsup * tpsup, (gsup + 1) * tpsup, 2):
                    s4_sqload(t0)
                    s4_sqload(t0 + 1)
                    s4_front(t0, gsup, only_mm, norm_only=("z" in stages))
                    s4_front(t0 + 1, gsup, only_mm, norm_only=("z" in stages))

            if do_back and gsup >= 1:
                g2 = gsup - 1
                for t0 in range(g2 * tpsup, (g2 + 1) * tpsup, 2):
                    s4_x1load(t0)
                    s4_x1load(t0 + 1)
                    s4_back(t0, g2)
                    s4_back(t0 + 1, g2)
    p.barrier()
    s4.close()

    if "5" not in stages:
        p.wait_all("sp")
        p.emit(nc, root)
        root.close()
        nc.finalize()
        return nc
    s5 = ExitStack()
    wffn_sb5 = load_weight_scratch_bf16(s5, "w_s5", s_wffnq, D)
    ten5 = mk_front_tensors(s5, "s5")
    xqT5 = s5.enter_context(nc.sbuf_tensor("s5xqT", [128, NCHUNK, 128], BF16))
    ptb5 = s5.enter_context(nc.psum_tensor("s5ptb", [128, 1024], BF16))
    ob5 = [s5.enter_context(nc.sbuf_tensor(f"s5o{j}", [128, D], F32)) for j in range(2)]
    psum5 = s5.enter_context(nc.psum_tensor("s5p", [128, D], F32))

    for t in range(0, 2 * tpsup):
        quant_load(t, ten5, "s5", s_x2, "lx2")
    for t in range(0, tpsup):
        quant_A(t, ten5, "s5", s_x2, "lx2")
    quant_B(0, ten5, sQ2, "s5")
    pend5 = []
    for s in range(nSup):
        for t0 in range(s * tpsup, (s + 1) * tpsup, 2):
            quant_C(t0, ten5, "s5")
            quant_C(t0 + 1, ten5, "s5")
            if pend5:
                pt0 = pend5.pop(0)
                for t in (pt0, pt0 + 1):
                    ffn_local(t, ten5, xqT5, ptb5, "w_s5", wffn_sb5, psum5,
                              ob5[t % 2], sQ2, "s5", out, "sout")
                for t in (pt0 + 2 * tpsup, pt0 + 2 * tpsup + 1):
                    if t < nT:
                        quant_load(t, ten5, "s5", s_x2, "lx2")
            if s + 1 < nSup:
                for t in (t0 + tpsup, t0 + tpsup + 1):
                    quant_A(t, ten5, "s5", s_x2, "lx2")
            pend5.append(t0)
        if s + 1 < nSup:
            quant_B(s + 1, ten5, sQ2, "s5")
    while pend5:
        pt0 = pend5.pop(0)
        for t in (pt0, pt0 + 1):
            ffn_local(t, ten5, xqT5, ptb5, "w_s5", wffn_sb5, psum5,
                      ob5[t % 2], sQ2, "s5", out, "sout")
    p.wait_all("sp")
    s5.close()
    p.emit(nc, root)
    root.close()
    nc.finalize()
    return nc


def kernel(L0_attn_hidden_states, L0_attn_Wq, L0_attn_bq, L0_attn_Wk, L0_attn_bk,
           L0_attn_Wv, L0_attn_bv, L0_attn_Wo, L0_ffn_weight):
    x = np.ascontiguousarray(np.asarray(L0_attn_hidden_states, dtype=np.float32))
    B, S, _ = x.shape
    per = B // N_CORES
    NT = per * S
    nc = build_program(NT=NT, SEQ=S, SUP=512)
    common = {
        "wffnT": np.ascontiguousarray(np.asarray(L0_ffn_weight, np.float32).T),
        "wqT": np.ascontiguousarray(np.asarray(L0_attn_Wq, np.float32).T),
        "wkT": np.ascontiguousarray(np.asarray(L0_attn_Wk, np.float32).T),
        "wvT": np.ascontiguousarray(np.asarray(L0_attn_Wv, np.float32).T),
        "woT": np.ascontiguousarray(np.asarray(L0_attn_Wo, np.float32).T),
        "bq": np.ascontiguousarray(np.asarray(L0_attn_bq, np.float32).reshape(1, -1)),
        "bk": np.ascontiguousarray(np.asarray(L0_attn_bk, np.float32).reshape(1, -1)),
        "bv": np.ascontiguousarray(np.asarray(L0_attn_bv, np.float32).reshape(1, -1)),
    }
    in_maps = []
    for c in range(N_CORES):
        m = dict(common)
        m["x"] = np.ascontiguousarray(x[c * per:(c + 1) * per].reshape(NT, 2048))
        in_maps.append(m)
    res = run_bass_kernel_spmd(nc, in_maps, core_ids=list(range(N_CORES)))
    outs = [np.asarray(res.results[c]["out"]).reshape(per, S, 2048)
            for c in range(N_CORES)]
    return np.concatenate(outs, axis=0)
